# version 12
# baseline (speedup 1.0000x reference)
import math
from contextlib import ExitStack

import numpy as np
import ml_dtypes

import concourse.bass as bass
import concourse.mybir as mybir
from concourse.bass_utils import run_bass_kernel_spmd

F32 = mybir.dt.float32
BF16 = mybir.dt.bfloat16
AF = mybir.ActivationFunctionType
OP = mybir.AluOpType
BFNP = ml_dtypes.bfloat16

NCORES = 8
T = 2048
NB = 4
BLK = 512
NTC = 16
E = 1024
H = 16
D = 64
MF = 256
MEPS = 256.0 * 1e-6
LNEPS = 1e-5
SWAP32 = [i ^ 1 for i in range(32)]


def _hadamard(n):
    m = np.array([[1.0]], dtype=np.float32)
    while m.shape[0] < n:
        m = np.block([[m, m], [m, -m]])
    return m


def _host_consts():
    c = {}
    H64 = _hadamard(64)
    c["h64"] = np.vstack([H64, H64]).astype(BFNP)
    I64 = np.eye(64, dtype=np.float32)
    rep = np.hstack([I64] * 8)
    c["i64rep"] = np.vstack([rep, rep]).astype(np.float32)
    c["idnb"] = np.eye(128, dtype=np.float32).astype(BFNP)
    c["idnf"] = np.eye(128, dtype=np.float32)
    selnq = np.zeros((128, 128), dtype=np.float32)
    for dc in range(8):
        selnq[0:64, 16 * dc + 2 * dc] = 1.0
        selnq[64:128, 16 * dc + 2 * dc + 1] = 1.0
    c["selnq"] = selnq.astype(BFNP)
    selg = np.zeros((16, 1024), dtype=np.float32)
    for p in range(8):
        selg[2 * p, 128 * p:128 * p + 64] = 1.0
        selg[2 * p + 1, 128 * p + 64:128 * p + 128] = 1.0
    c["selg"] = selg.astype(BFNP)
    return c


def _rope_tables(t0):
    d = np.arange(D)
    freqs = 10000.0 ** (-(2 * (d // 2)) / D)
    g = np.arange(t0, t0 + T, dtype=np.float64)
    ang = np.where(g[None, :] > 0, (g[None, :] - 1) * freqs[:, None], 0.0)
    Ct = np.where(g[None, :] > 0, np.cos(ang), 1.0)
    sgn = np.where(d % 2 == 0, -1.0, 1.0)[:, None]
    St = np.where(g[None, :] > 0, sgn * np.sin(ang), 0.0)
    return (np.vstack([Ct, Ct]).astype(BFNP), np.vstack([St, St]).astype(BFNP))


class _Ctr:
    def __init__(self, ctx, nc, name):
        self.sem = ctx.enter_context(nc.semaphore(name))
        self.n = 0

    def inc(self, instr, k=1):
        instr.then_inc(self.sem, k)
        self.n += k
        return self.n


def build_nc(debug=False):
    nc = bass.Bass("TRN2", target_bir_lowering=False)

    def din(name, shape, dtype=F32):
        return nc.dram_tensor(name, shape, dtype, kind="ExternalInput").ap()

    X = din("x", [T, E])
    WQ = din("wq", [E, E]); WK = din("wk", [E, E]); WV = din("wv", [E, E])
    WO = din("wo", [E, E])
    WR = din("wr", [E, E]); WKF = din("wkf", [E, E]); WVF = din("wvf", [E, E])
    MUR = din("mur", [128, 8]); MUK = din("muk", [128, 8])
    SG = [din(f"sg{s}", [64, 128]) for s in (1, 2, 3)]
    SCL = din("scl", [64, 128])
    H64 = din("h64", [128, 64], BF16)
    I64R = din("i64rep", [128, 512])
    IDNB = din("idnb", [128, 128], BF16)
    IDNF = din("idnf", [128, 128])
    SELNQ = din("selnq", [128, 128], BF16)
    SELG = din("selg", [16, 1024], BF16)
    ROPEC = din("ropec", [128, T], BF16)
    ROPES = din("ropes", [128, T], BF16)
    HSEL = din("hsel", [8, 1])
    OUT = nc.dram_tensor("out", [T, E], F32, kind="ExternalOutput").ap()

    YBUF = nc.dram_tensor("ybuf", [T, E], F32, kind="Internal")
    SBIN = nc.dram_tensor("sbin", [H * 128, MF], BF16, kind="Internal")
    SBOUT = nc.dram_tensor("sbout", [H * 128, MF], BF16, kind="Internal")
    GIN = nc.dram_tensor("gin", [1, E], F32, kind="Internal")
    GOUT = nc.dram_tensor("gout", [8, E], F32, kind="Internal")

    taps = {}
    if debug:
        def tap(name, shape, dtype=F32):
            taps[name] = nc.dram_tensor("tap_" + name, shape, dtype,
                                        kind="ExternalOutput").ap()
        tap("mt", [128, 4096], BF16); tap("xT0", [128, T], BF16)
        tap("krot0", [128, BLK], BF16); tap("kphi00", [128, 4096], BF16)
        tap("w00", [128, 16]); tap("saug0", [128, 1040], BF16)
        tap("enq0", [16, BLK]); tap("g0", [16, BLK], BF16)
        tap("oh0", [128, BLK], BF16); tap("y0", [128, E])
        tap("xr0", [128, BLK], BF16); tap("kact0", [128, BLK], BF16)
        tap("z20", [128, E])

    ctx = ExitStack()

    def sb(name, shape, dtype):
        return ctx.enter_context(nc.sbuf_tensor(name, shape, dtype))

    def psa(name, shape, dtype=F32):
        return ctx.enter_context(nc.psum_tensor(name, shape, dtype))

    XTB = sb("xtb_s", [128, 8 * T], BF16)
    xT = [XTB[:, T * k:T * (k + 1)] for k in range(8)]
    yT = xT
    xb = [sb(f"xb{t}", [128, E], BF16) for t in range(8)]
    WA = [sb(f"wA{k}", [128, E], BF16) for k in range(8)]
    WB = [sb(f"wB{k}", [128, E], BF16) for k in range(8)]
    mt = sb("mt_s", [128, 4096], BF16)
    ropec = sb("ropec_s", [128, T], BF16)
    ropes = sb("ropes_s", [128, T], BF16)
    idnb = sb("idnb_s", [128, 128], BF16)
    idnf = sb("idnf_s", [128, 128], F32)
    selnq = sb("selnq_s", [128, 128], BF16)
    selg = sb("selg_s", [16, 1024], BF16)
    h64 = sb("h64_s", [128, 64], BF16)
    i64r = sb("i64r", [128, 512], F32)
    sg = [sb(f"sgs{s}", [64, 128], F32) for s in range(3)]
    scl = sb("scl_s", [64, 128], F32)
    sgT = [sb(f"sgT{s}", [128, 64], F32) for s in range(3)]
    gT = sb("gT", [128, 64], F32)
    r0 = sb("r0", [128, 512], BF16)
    rt = sb("rt", [128, 512], BF16)
    mur = sb("mur_s", [128, 8], F32)
    muk = sb("muk_s", [128, 8], F32)
    hsel = sb("hsel_s", [8, 1], F32)

    ksb = [sb(f"ksb{i}", [128, BLK], F32) for i in range(2)]
    krot = [sb(f"krot{d}", [128, BLK], BF16) for d in range(8)]
    xr = krot
    kr2 = [sb(f"kr2{i}", [128, BLK], BF16) for i in range(2)]
    nkc = sb("nkc", [16, BLK], F32)
    wnk = [sb(f"wnk{t}", [128, 16], F32) for t in range(4)]
    AR1 = sb("AR1", [128, 16384], BF16)
    kphi = [AR1[:, 4096 * t:4096 * (t + 1)] for t in range(4)]
    qexp = [AR1[:, 512 * i:512 * (i + 1)] for i in range(32)]
    xk = [AR1[:, 512 * k:512 * (k + 1)] for k in range(8)]
    kact = [AR1[:, 4096 + 512 * k:4096 + 512 * (k + 1)] for k in range(8)]
    WC = [AR1[:, 8192 + 1024 * k:8192 + 1024 * (k + 1)] for k in range(8)]
    SSA = sb("SSA", [128, 4096], F32)
    Ssb = [SSA[:, MF * h:MF * (h + 1)] for h in range(H)]
    z1 = [SSA[:, 1024 * i:1024 * (i + 1)] for i in range(2)]
    y2 = [SSA[:, 1024 * i:1024 * (i + 1)] for i in range(2)]
    sqs = SSA[:, 2048:3072]
    SSAB16 = SSA[:, :].bitcast(BF16)
    VAA = sb("VAA", [128, 4160], BF16)
    vaug = [VAA[:, 1040 * t:1040 * (t + 1)] for t in range(4)]
    VAF = VAA[:, :].bitcast(F32)
    xf2 = [VAF[:, 1024 * i:1024 * (i + 1)] for i in range(2)]
    osb = [VAF[:, 1024 * i:1024 * (i + 1)] for i in range(2)]
    saug = [sb(f"saug{m}", [128, 1040], BF16) for m in range(2)]
    skc = [sb(f"skc{m}", [128, 16], BF16) for m in range(2)]
    GAR = sb("GAR", [128, 2304], F32)
    GARB = GAR[:, :].bitcast(BF16)
    enq = GAR[0:16, 0:512]
    gch = [GAR[0:16, 512 + 512 * i:1024 + 512 * i] for i in range(2)]
    gsb = GARB[0:16, 3072:3584]
    gx = [GARB[:, 3584 + 512 * i:4096 + 512 * i] for i in range(2)]
    tmul = [GAR[:, 1024 * i:1024 * (i + 1)] for i in range(2)]
    dsb = GARB[:, 4096:4608]
    OHA = sb("OHA", [128, 4096], BF16)
    oh = [OHA[:, 512 * p:512 * (p + 1)] for p in range(8)]
    rsb = [OHA[:, 1024 * i:1024 * (i + 1)] for i in range(2)]
    OHAF = OHA[:, :].bitcast(F32)
    agsb = OHAF[0:8, 1024:2048]
    hrow = SSAB16[0:1, 6144:7168]
    stat = sb("stat", [128, 8], F32)
    ytr = [sb(f"ytr{i}", [128, E], F32) for i in range(2)]
    z2 = ytr[0:2]
    ytr.append(SSA[:, 3072:4096])
    haloT = sb("haloT", [128, 8], BF16)
    ones11 = sb("ones11", [1, 1], BF16)

    PS = [psa(f"PS{i}", [128, 512], F32) for i in range(6)]
    PTR = psa("PTR", [128, 2048], BF16)

    cW = _Ctr(ctx, nc, "cW")
    cX = _Ctr(ctx, nc, "cX")
    cT = _Ctr(ctx, nc, "cT")
    cV = _Ctr(ctx, nc, "cV")
    cS = _Ctr(ctx, nc, "cS")
    cG = _Ctr(ctx, nc, "cG")
    cC = _Ctr(ctx, nc, "cC")
    cXF = [_Ctr(ctx, nc, f"cXF{i}") for i in range(2)]
    cYD = [_Ctr(ctx, nc, f"cYD{i}") for i in range(2)]
    cY2 = [_Ctr(ctx, nc, f"cY2{i}") for i in range(2)]
    cOD = [_Ctr(ctx, nc, f"cOD{i}") for i in range(2)]

    block = ctx.enter_context(nc.Block())
    marks = {}

    def mark(name, ctr):
        marks[name] = ctr.n

    def vchain(v, instr):
        cV.inc(instr)
        v.wait_ge(cV.sem, cV.n)

    @block.gpsimd
    def _(g):
        for s, dst in ((IDNB, idnb), (IDNF, idnf), (SELNQ, selnq),
                       (SELG, selg), (H64, h64), (I64R, i64r),
                       (ROPEC, ropec), (ROPES, ropes),
                       (MUR, mur), (MUK, muk), (HSEL, hsel),
                       (SG[0], sg[0]), (SG[1], sg[1]), (SG[2], sg[2]), (SCL, scl)):
            g.dma_start(out=dst[:], in_=s[:]).then_inc(cW.sem, 16)
            cW.n += 16
        mark("consts", cW)
        g.wait_ge(cW.sem, cW.n)
        for t in range(8):
            g.dma_start(out=xb[t][:], in_=X[128 * t:128 * (t + 1), :]).then_inc(cW.sem, 16)
            cW.n += 16
        mark("xbg_0", cW)
        g.wait_ge(cW.sem, cW.n)
        for k in range(8):
            g.dma_start(out=WA[k][:], in_=WK[128 * k:128 * (k + 1), :]).then_inc(cW.sem, 16)
            cW.n += 16
        mark("wk", cW)
        g.wait_ge(cW.sem, cW.n)
        for k in range(8):
            g.dma_start(out=WB[k][:], in_=WV[128 * k:128 * (k + 1), :]).then_inc(cW.sem, 16)
            cW.n += 16
        mark("wv", cW)
        g.memset(ones11[:], 1.0)
        for m in range(2):
            g.memset(skc[m][:], 0.0)
        for h in range(H):
            g.memset(Ssb[h][64:128, :], 0.0)
        g.memset(haloT[:], 0.0).then_inc(cG.sem, 1)
        cG.n += 1
        mark("memsets", cG)

    @block.tensor
    def _(t):
        t.wait_ge(cW.sem, marks["consts"])
        t.wait_ge(cG.sem, marks["memsets"])
        for s in range(3):
            t.transpose(PS[s][0:128, 0:64], sg[s][:, :], idnf[0:64, 0:64])
        cT.inc(t.transpose(PS[3][0:128, 0:64], scl[:, :], idnf[0:64, 0:64]))
        mark("sgT_ps", cT)

    @block.vector
    def _(v):
        v.wait_ge(cT.sem, marks["sgT_ps"])
        for s in range(3):
            v.tensor_copy(sgT[s][:, :], PS[s][0:128, 0:64])
        cV.inc(v.tensor_scalar(gT[:, :], PS[3][0:128, 0:64], 1.0 / 4096.0,
                               None, OP.mult))
        mark("sgT", cV)

    for b in range(8):
        @block.vector
        def _(v, b=b):
            v.wait_ge(cV.sem, marks["sgT"])
            if b >= 1:
                v.wait_ge(cT.sem, marks[f"mb_mm2_{b-1}"])
            gbc = gT[:, 8 * b:8 * b + 8].unsqueeze(2).broadcast_to([128, 8, 64])
            cV.inc(v.tensor_tensor(r0[:, :], i64r[:, :], gbc, OP.mult))
            mark(f"mb_r0_{b}", cV)

        @block.tensor
        def _(t, b=b):
            t.wait_ge(cV.sem, marks[f"mb_r0_{b}"])
            t.matmul(PS[4][0:64, :], h64[0:64, :], r0[0:64, :], start=True, stop=True)
            cT.inc(t.matmul(PS[4][64:128, :], h64[64:128, :], r0[64:128, :],
                            start=True, stop=True, tile_position=(64, 64)))
            mark(f"mb_mm0_{b}", cT)

        @block.vector
        def _(v, b=b):
            v.wait_ge(cT.sem, marks[f"mb_mm0_{b}"])
            sbc = sgT[2][:, 8 * b:8 * b + 8].unsqueeze(2).broadcast_to([128, 8, 64])
            cV.inc(v.tensor_tensor(rt[:, :], PS[4][:, :], sbc, OP.mult))
            mark(f"mb_r3_{b}", cV)

        @block.tensor
        def _(t, b=b):
            t.wait_ge(cV.sem, marks[f"mb_r3_{b}"])
            t.matmul(PS[5][0:64, :], h64[0:64, :], rt[0:64, :], start=True, stop=True)
            cT.inc(t.matmul(PS[5][64:128, :], h64[64:128, :], rt[64:128, :],
                            start=True, stop=True, tile_position=(64, 64)))
            mark(f"mb_mm1_{b}", cT)

        @block.vector
        def _(v, b=b):
            v.wait_ge(cT.sem, marks[f"mb_mm1_{b}"])
            sbc = sgT[1][:, 8 * b:8 * b + 8].unsqueeze(2).broadcast_to([128, 8, 64])
            cV.inc(v.tensor_tensor(rt[:, :], PS[5][:, :], sbc, OP.mult))
            mark(f"mb_r2_{b}", cV)

        @block.tensor
        def _(t, b=b):
            t.wait_ge(cV.sem, marks[f"mb_r2_{b}"])
            t.matmul(PS[4][0:64, :], h64[0:64, :], rt[0:64, :], start=True, stop=True)
            cT.inc(t.matmul(PS[4][64:128, :], h64[64:128, :], rt[64:128, :],
                            start=True, stop=True, tile_position=(64, 64)))
            mark(f"mb_mm2_{b}", cT)

        @block.vector
        def _(v, b=b):
            v.wait_ge(cT.sem, marks[f"mb_mm2_{b}"])
            sbc = sgT[0][:, 8 * b:8 * b + 8].unsqueeze(2).broadcast_to([128, 8, 64])
            cV.inc(v.tensor_tensor(mt[:, 512 * b:512 * b + 512], PS[4][:, :],
                                   sbc, OP.mult))
            mark(f"mb_done_{b}", cV)
    mark("mt_done", cV)

    for grp in range(2):
        if grp == 1:
            @block.gpsimd
            def _(g):
                g.wait_ge(cT.sem, marks["xtr_7"])
                g.wait_ge(cW.sem, cW.n)
                for t8 in range(8):
                    g.dma_start(out=xb[t8][:],
                                in_=X[128 * (8 + t8):128 * (9 + t8), :]
                                ).then_inc(cW.sem, 16)
                    cW.n += 16
                mark("xbg_1", cW)
        for kc in range(8):
            idx = grp * 8 + kc
            half = idx % 2

            @block.tensor
            def _(t, grp=grp, kc=kc, idx=idx, half=half):
                if kc == 0:
                    t.wait_ge(cW.sem, marks[f"xbg_{grp}"])
                if idx >= 2:
                    t.wait_ge(cV.sem, marks[f"xtev_{idx-2}"])
                last = None
                for tc8 in range(8):
                    last = t.transpose(PTR[:, 1024 * half + 128 * tc8:
                                           1024 * half + 128 * tc8 + 128],
                                       xb[tc8][:, 128 * kc:128 * kc + 128],
                                       idnb[:, :])
                cT.inc(last)
                mark(f"xtr_{idx}", cT)

            @block.vector
            def _(v, kc=kc, grp=grp, idx=idx, half=half):
                v.wait_ge(cT.sem, marks[f"xtr_{idx}"])
                cV.inc(v.tensor_copy(xT[kc][:, 1024 * grp:1024 * grp + 1024],
                                     PTR[:, 1024 * half:1024 * half + 1024]))
                mark(f"xtev_{idx}", cV)
    mark("xT_done", cV)

    def emit_qk_side(b, base, c0):
        kside = (base == 0)

        def emit_phi_round(dc, r):
            i = base + 8 * b + dc

            @block.tensor
            def _(t, dc=dc, i=i, r=r, b=b):
                t.wait_ge(cG.sem, marks[f"grot_{i}"])
                if not kside and b > 0 and dc == 0 and r == 0:
                    t.wait_ge(cV.sem, marks[f"z1_{4*(b-1)+2}"])
                pr = 2 * i + r - 1
                if pr >= 2 * base:
                    t.wait_ge(cS.sem, marks[f"per_{pr}"])
                m = None
                for par in range(2):
                    h = 2 * dc + par
                    lo = 64 * par
                    if kside:
                        for tcs in range(2):
                            tc = 2 * r + tcs
                            m = t.matmul(PS[2 + par][:, 256 * tcs:256 * tcs + 256],
                                         krot[dc][lo:lo + 64, 128 * tc:128 * tc + 128],
                                         mt[lo:lo + 64, 256 * h:256 * h + 256],
                                         start=True, stop=True)
                    else:
                        m = t.matmul(PS[2 + par][:, :],
                                     mt[lo:lo + 64, 256 * h + 128 * r:
                                        256 * h + 128 * r + 128],
                                     krot[dc][lo:lo + 64, :],
                                     start=True, stop=True)
                cT.inc(m)
                mark(f"phr_{2*i+r}", cT)

            @block.scalar
            def _(sc, dc=dc, i=i, r=r):
                sc.wait_ge(cT.sem, marks[f"phr_{2*i+r}"])
                last = None
                for par in range(2):
                    h = 2 * dc + par
                    if kside:
                        dv = AR1[:, 4096 * 2 * r + 256 * h:
                                 4096 * 2 * r + 256 * h + 4352]
                        dv = dv.rearrange("p (a c) -> p a c", a=17)[:, 0:17:16, :]
                        last = sc.activation(
                            dv,
                            PS[2 + par][:, :].rearrange("p (a c) -> p a c", a=2),
                            AF.Exp)
                    else:
                        qi = 2 * (2 * dc + par) + r
                        last = sc.activation(qexp[qi][:, :], PS[2 + par][:, :],
                                             AF.Exp)
                cS.inc(last)
                mark(f"per_{2*i+r}", cS)
            if r == 1:
                marks[f"pe_{i}"] = marks[f"per_{2*i+1}"]

        pending = []
        for dc in range(8):
            i = base + 8 * b + dc

            @block.tensor
            def _(t, b=b, dc=dc, c0=c0, i=i):
                if kside and b == 0 and dc == 0:
                    t.wait_ge(cW.sem, marks["wk"])
                    t.wait_ge(cV.sem, marks["xT_done"])
                if not kside and b == 0 and dc == 0:
                    t.wait_ge(cW.sem, marks["wq"])
                if dc < 2:
                    if b > 0:
                        t.wait_ge(cS.sem, marks[f"vev_{4*(b-1)+3}"] if kside
                                  else marks[f"pe_{base + 8*(b-1) + 7}"])
                        if not kside:
                            t.wait_ge(cV.sem, marks[f"z1_{4*(b-1)+1}"])
                    elif not kside:
                        t.wait_ge(cS.sem, marks["vev_15"])
                        t.wait_ge(cV.sem, marks["sfold_63"])
                if (b, dc) > (0, 1):
                    t.wait_ge(cV.sem, marks[f"rope_{i-2}"])
                m = None
                for kc in range(8):
                    m = t.matmul(PS[dc % 2][:, :], WA[kc][:, 128 * dc:128 * dc + 128],
                                 xT[kc][:, c0:c0 + BLK],
                                 start=(kc == 0), stop=(kc == 7))
                cT.inc(m)
                mark(f"kmm_{i}", cT)

            if pending:
                emit_phi_round(*pending.pop(0))

            @block.vector
            def _(v, dc=dc, c0=c0, i=i):
                v.wait_ge(cT.sem, marks[f"kmm_{i}"])
                if i >= 2:
                    v.wait_ge(cG.sem, marks[f"grot_{i-2}"])
                vchain(v, v.stream_shuffle(ksb[dc % 2][:, :], PS[dc % 2][:, :],
                                           SWAP32))
                cV.inc(v.tensor_tensor(kr2[dc % 2][:, :], PS[dc % 2][:, :],
                                       ropec[:, c0:c0 + BLK], OP.mult))
                mark(f"rope_{i}", cV)

            @block.gpsimd
            def _(g, dc=dc, c0=c0, i=i):
                g.wait_ge(cV.sem, marks[f"rope_{i}"])
                cG.inc(g.tensor_tensor(krot[dc][:, :], ksb[dc % 2][:, :],
                                       ropes[:, c0:c0 + BLK], OP.mult))
                g.wait_ge(cG.sem, cG.n)
                cG.inc(g.tensor_tensor(krot[dc][:, :], krot[dc][:, :],
                                       kr2[dc % 2][:, :], OP.add))
                mark(f"grot_{i}", cG)

            if len(pending) > 2:
                emit_phi_round(*pending.pop(0))
            pending.append((dc, 0))
            pending.append((dc, 1))
        while pending:
            emit_phi_round(*pending.pop(0))

        for dc in range(8):
            i = base + 8 * b + dc

            @block.vector
            def _(v, dc=dc, i=i):
                if dc == 0:
                    v.wait_ge(cV.sem, cV.n)
                if dc < 2:
                    v.wait_ge(cG.sem, marks[f"grot_{i - dc + 7}"])
                if dc >= 2:
                    v.wait_ge(cT.sem, marks[f"nmm_{i-2}"])
                cV.inc(v.tensor_tensor(kr2[dc % 2][:, :], krot[dc][:, :],
                                       krot[dc][:, :], OP.mult))
                mark(f"ksq_{i}", cV)

            @block.tensor
            def _(t, dc=dc, i=i, b=b):
                t.wait_ge(cV.sem, marks[f"ksq_{i}"])
                if dc == 0:
                    if kside and b > 0:
                        t.wait_ge(cV.sem, marks[f"sfold_{16*(b-1)+15}"])
                    if not kside:
                        if b > 0:
                            t.wait_ge(cV.sem, marks[f"z1_{4*(b-1)+3}"])
                        else:
                            t.wait_ge(cV.sem, marks["sfold_63"])
                cT.inc(t.matmul(PS[4][0:16, :], selnq[:, 16 * dc:16 * dc + 16],
                                kr2[dc % 2][:, :], start=(dc == 0), stop=(dc == 7),
                                skip_group_check=True))
                mark(f"nmm_{i}", cT)

    for b in range(NB):
        c0 = BLK * b
        emit_qk_side(b, 0, c0)

        @block.vector
        def _(v, b=b):
            v.wait_ge(cT.sem, marks[f"nmm_{8*b+7}"])
            cV.inc(v.tensor_copy(nkc[:, :], PS[4][0:16, :]))
            mark(f"nkc_{b}", cV)

        for tc in range(4):
            @block.tensor
            def _(t, b=b, tc=tc):
                t.wait_ge(cV.sem, marks[f"nkc_{b}"])
                if b > 0 and tc == 0:
                    t.wait_ge(cV.sem, marks[f"sfold_{16*(b-1)+15}"])
                cT.inc(t.transpose(PS[5][0:128, 32 * tc:32 * tc + 16],
                                   nkc[0:16, 128 * tc:128 * tc + 128],
                                   idnf[0:16, 0:16]))
                mark(f"nkt_{b}_{tc}", cT)

        @block.scalar
        def _(sc, b=b):
            sc.wait_ge(cT.sem, marks[f"nkt_{b}_3"])
            last = None
            for tc in range(4):
                last = sc.activation(wnk[tc][:, 0:16],
                                     PS[5][0:128, 32 * tc:32 * tc + 16],
                                     AF.Exp, bias=0.0, scale=-0.5)
            cS.inc(last)
            mark(f"wnk_{b}", cS)

        for tc in range(4):
            @block.tensor
            def _(t, b=b, tc=tc, c0=c0):
                i = 4 * b + tc
                reg = 2 * (tc % 2)
                if b == 0 and tc == 0:
                    t.wait_ge(cW.sem, marks["wv"])
                t.wait_ge(cS.sem, marks[f"wnk_{b}"])
                if i >= 2:
                    t.wait_ge(cV.sem, marks[f"vaug_{i-2}"])
                tok = c0 + 128 * tc
                m = None
                for kc in range(8):
                    t.matmul(PS[reg][:, :], xT[kc][:, tok:tok + 128],
                             WB[kc][:, 0:512], start=(kc == 0), stop=(kc == 7))
                for kc in range(8):
                    m = t.matmul(PS[reg + 1][:, :], xT[kc][:, tok:tok + 128],
                                 WB[kc][:, 512:1024], start=(kc == 0), stop=(kc == 7))
                cT.inc(m)
                mark(f"vmm_{i}", cT)

            @block.scalar
            def _(sc, b=b, tc=tc):
                i = 4 * b + tc
                reg = 2 * (tc % 2)
                sc.wait_ge(cT.sem, marks[f"vmm_{i}"])
                d0 = vaug[tc][:, 0:1040].rearrange("p (h c) -> p h c", h=16)
                sc.copy(d0[:, 0:8, 0:64],
                        PS[reg][:, :].rearrange("p (h c) -> p h c", h=8))
                cS.inc(sc.copy(d0[:, 8:16, 0:64],
                               PS[reg + 1][:, :].rearrange("p (h c) -> p h c", h=8)))
                mark(f"vev_{i}", cS)

            @block.vector
            def _(v, b=b, tc=tc):
                i = 4 * b + tc
                v.wait_ge(cS.sem, marks[f"vev_{i}"])
                v.wait_ge(cS.sem, marks[f"wnk_{b}"])
                va = vaug[tc][:, 0:1040].rearrange("p (h c) -> p h c", h=16)
                vchain(v, v.memset(va[:, :, 64:65], 1.0))
                wb = wnk[tc][:, 0:16].unsqueeze(2).broadcast_to([128, 16, 65])
                cV.inc(v.tensor_tensor(va, va, wb, OP.mult))
                mark(f"vaug_{i}", cV)

        for h in range(H):
            i = 16 * b + h
            reg = h % 2

            @block.tensor
            def _(t, b=b, h=h, i=i, reg=reg):
                if h < 2:
                    t.wait_ge(cV.sem, marks[f"vaug_{4*b+3}"])
                    if b > 0:
                        t.wait_ge(cV.sem, marks[f"sfold_{16*(b-1)+14+h}"])
                else:
                    t.wait_ge(cV.sem, marks[f"sfold_{i-2}"])
                m = None
                for tc in range(4):
                    m = t.matmul(PS[4 + reg][0:65, 0:256],
                                 vaug[tc][:, 65 * h:65 * h + 65],
                                 kphi[tc][:, 256 * h:256 * h + 256],
                                 start=(tc == 0), stop=(tc == 3))
                cT.inc(m)
                mark(f"smm_{i}", cT)

            @block.vector
            def _(v, b=b, h=h, i=i, reg=reg):
                v.wait_ge(cT.sem, marks[f"smm_{i}"])
                if b == 0:
                    cV.inc(v.tensor_copy(Ssb[h][0:65, :], PS[4 + reg][0:65, 0:256]))
                else:
                    cV.inc(v.tensor_tensor(Ssb[h][0:65, :], PS[4 + reg][0:65, 0:256],
                                           Ssb[h][0:65, :], OP.add))
                mark(f"sfold_{i}", cV)
    mark("S_done", cV)

    @block.gpsimd
    def _(g):
        g.wait_ge(cT.sem, marks["kmm_31"])
        g.wait_ge(cW.sem, cW.n)
        for k in range(8):
            g.dma_start(out=WA[k][:], in_=WQ[128 * k:128 * (k + 1), :]).then_inc(cW.sem, 16)
            cW.n += 16
        mark("wq", cW)
        g.wait_ge(cT.sem, marks["vmm_15"])
        g.wait_ge(cW.sem, cW.n)
        for k in range(8):
            g.dma_start(out=WB[k][:], in_=WO[128 * k:128 * (k + 1), :]).then_inc(cW.sem, 16)
            cW.n += 16
        mark("wo", cW)
        g.wait_ge(cV.sem, marks["S_done"])
        g.wait_ge(cW.sem, cW.n)
        for h in range(H):
            g.dma_start(out=SBIN.ap()[128 * h:128 * (h + 1), :],
                        in_=Ssb[h][:, :]).then_inc(cW.sem, 16)
            cW.n += 16
        mark("sbin", cW)
        g.wait_ge(cW.sem, marks["sbin"])
        cC.inc(g.collective_compute(
            "AllReduce", OP.add,
            replica_groups=[[0, 1, 2, 3], [4, 5, 6, 7]],
            ins=[SBIN.ap().opt()], outs=[SBOUT.ap().opt()]))
        mark("s_ar", cC)

    def emit_sred():
        @block.gpsimd
        def _(g):
            g.wait_ge(cC.sem, marks["s_ar"])
            g.wait_ge(cW.sem, cW.n)
            for h in range(H):
                g.dma_start(out=Ssb[h][:, :],
                            in_=SBOUT.ap()[128 * h:128 * (h + 1), :]
                            ).then_inc(cW.sem, 16)
                cW.n += 16
            mark("sred", cW)

    PTF2A = PTR[:, :].bitcast(F32)

    def emit_ytt(i):
        @block.tensor
        def _(t, i=i):
            t.wait_ge(cV.sem, marks[f"lnapply_{i}"])
            if i >= 1:
                t.wait_ge(cV.sem, marks[f"ytev2_{i-1}"])
            else:
                t.wait_ge(cV.sem, marks["xT_done"])
            last = None
            for kc in range(8):
                last = t.transpose(
                    PTF2A[:, 128 * kc:128 * kc + 128],
                    ytr[i % 3][:, 128 * kc:128 * kc + 128], idnf[:, :])
            cT.inc(last)
            mark(f"ytt_{i}", cT)

        @block.vector
        def _(v, i=i):
            v.wait_ge(cT.sem, marks[f"ytt_{i}"])
            cV.inc(v.tensor_copy(
                XTB.rearrange("p (a t) -> p a t", a=8)[:, 0:8,
                                                       128 * i:128 * i + 128],
                PTF2A[:, 0:1024].rearrange("p (a t) -> p a t", a=8)))
            mark(f"ytev2_{i}", cV)

    for b in range(NB):
        c0 = BLK * b
        emit_qk_side(b, 32, c0)
        if b == 0:
            emit_sred()
        if b == 3:
            @block.gpsimd
            def _(g):
                g.wait_ge(cT.sem, marks["kmm_63"])
                g.wait_ge(cW.sem, cW.n)
                for k in range(8):
                    g.dma_start(out=WA[k][:],
                                in_=WR[128 * k:128 * (k + 1), :]).then_inc(cW.sem, 16)
                    cW.n += 16
                mark("wr", cW)

        if b == 0:
            for h in range(H):
                @block.tensor
                def _(t, h=h):
                    if h == 0:
                        t.wait_ge(cW.sem, marks["sred"])
                        t.wait_ge(cV.sem, marks["rope_39"])
                    if h >= 2:
                        t.wait_ge(cV.sem, marks[f"sev_{h-2}"])
                    t.transpose(PS[h % 2][:, 0:128], Ssb[h][:, 0:128], idnf[:, :])
                    cT.inc(t.transpose(PS[h % 2][:, 128:256], Ssb[h][:, 128:256],
                                       idnf[:, :]))
                    mark(f"str_{h}", cT)

                @block.vector
                def _(v, h=h):
                    v.wait_ge(cT.sem, marks[f"str_{h}"])
                    for m in range(2):
                        v.tensor_copy(saug[m][:, 65 * h:65 * h + 65],
                                      PS[h % 2][:, 128 * m:128 * m + 65])
                    v.tensor_copy(skc[0][:, h:h + 1], PS[h % 2][:, 64:65])
                    cV.inc(v.tensor_copy(skc[1][:, h:h + 1], PS[h % 2][:, 192:193]))
                    mark(f"sev_{h}", cV)
            mark("saug_done", cV)

        @block.scalar
        def _(sc, b=b):
            sc.wait_ge(cT.sem, marks[f"nmm_{32+8*b+7}"])
            if b > 0:
                sc.wait_ge(cV.sem, marks[f"g_{b-1}"])
            cS.inc(sc.activation(enq[:, :], PS[4][0:16, :], AF.Exp,
                                 bias=0.0, scale=-0.5))
            mark(f"enq_{b}", cS)

        @block.tensor
        def _(t, b=b):
            t.wait_ge(cS.sem, marks[f"pe_{32+8*b+7}"])
            t.wait_ge(cS.sem, marks[f"enq_{b}"])
            if b > 0:
                t.wait_ge(cV.sem, marks[f"z1_{4*(b-1)+3}"])
            else:
                t.wait_ge(cV.sem, marks["saug_done"])
            first = True
            m = None
            for h in range(H):
                for mc in range(2):
                    m = t.matmul(PS[5][0:16, :], skc[mc][:, :],
                                 qexp[2 * h + mc][:, :],
                                 start=first, stop=(h == H - 1 and mc == 1))
                    first = False
            cT.inc(m)
            mark(f"draw_{b}", cT)

        @block.vector
        def _(v, b=b):
            v.wait_ge(cT.sem, marks[f"draw_{b}"])
            v.wait_ge(cS.sem, marks[f"enq_{b}"])
            vchain(v, v.tensor_tensor(gch[0][:, :], PS[5][0:16, :], enq[:, :],
                                      OP.mult))
            vchain(v, v.tensor_scalar(gch[1][:, :], gch[0][:, :], MEPS, None,
                                      OP.max))
            vchain(v, v.reciprocal(gch[0][:, :], gch[1][:, :]))
            cV.inc(v.tensor_tensor(gsb[:, :], gch[0][:, :], enq[:, :], OP.mult))
            mark(f"g_{b}", cV)

        for p in range(8):
            i = 8 * b + p
            reg = p % 2

            @block.tensor
            def _(t, b=b, p=p, i=i, reg=reg):
                t.wait_ge(cV.sem, marks[f"g_{b}"])
                if p >= 2:
                    t.wait_ge(cV.sem, marks[f"oh_{i-2}"])
                elif b > 0:
                    t.wait_ge(cV.sem, marks[f"oh_{8*(b-1)+6+p}"])
                t.matmul(PS[2 + reg][:, :], selg[:, 128 * p:128 * p + 128],
                         gsb[:, :], start=True, stop=True)
                m = None
                for par in range(2):
                    h = 2 * p + par
                    lo = 64 * par
                    for mc in range(2):
                        m = t.matmul(PS[reg][lo:lo + 64, :],
                                     saug[mc][:, 65 * h:65 * h + 64],
                                     qexp[2 * h + mc][:, :],
                                     start=(mc == 0), stop=(mc == 1))
                cT.inc(m)
                mark(f"numer_{i}", cT)

            @block.scalar
            def _(sc, i=i, reg=reg):
                sc.wait_ge(cT.sem, marks[f"numer_{i}"])
                cS.inc(sc.copy(gx[reg][:, :], PS[2 + reg][:, :]))
                mark(f"gxe_{i}", cS)

            @block.vector
            def _(v, p=p, i=i, reg=reg):
                v.wait_ge(cS.sem, marks[f"gxe_{i}"])
                cV.inc(v.tensor_tensor(oh[p][:, :], PS[reg][:, :], gx[reg][:, :],
                                       OP.mult))
                mark(f"oh_{i}", cV)

        for tc in range(4):
            i = 4 * b + tc
            if i >= 3:
                emit_ytt(i - 3)

            @block.sync
            def _(s, b=b, tc=tc, i=i):
                if i < 2:
                    s.wait_ge(cT.sem, marks["smm_63"])
                else:
                    s.wait_ge(cV.sem, marks[f"z1_{i-2}"])
                s.wait_ge(cXF[i % 2].sem, cXF[i % 2].n)
                s.dma_start(out=xf2[tc % 2][:, :],
                            in_=X[BLK * b + 128 * tc:BLK * b + 128 * tc + 128, :]
                            ).then_inc(cXF[i % 2].sem, 16)
                cXF[i % 2].n += 16
                mark(f"xf_{i}", cXF[i % 2])

            @block.tensor
            def _(t, b=b, tc=tc, i=i):
                t.wait_ge(cV.sem, marks[f"oh_{8*b+7}"])
                reg = [4, 0, 2, 4][tc]
                if b == 0 and tc == 0:
                    t.wait_ge(cW.sem, marks["wo"])
                if tc == 0 and b > 0:
                    t.wait_ge(cV.sem, marks[f"z1_{4*(b-1)+3}"])
                elif tc == 3:
                    t.wait_ge(cV.sem, marks[f"z1_{4*b}"])
                m = None
                for p in range(8):
                    t.matmul(PS[reg][:, :], oh[p][:, 128 * tc:128 * tc + 128],
                             WB[p][:, 0:512], start=(p == 0), stop=(p == 7))
                for p in range(8):
                    m = t.matmul(PS[reg + 1][:, :], oh[p][:, 128 * tc:128 * tc + 128],
                                 WB[p][:, 512:1024], start=(p == 0), stop=(p == 7))
                cT.inc(m)
                mark(f"oproj_{i}", cT)

            @block.vector
            def _(v, tc=tc, i=i):
                v.wait_ge(cT.sem, marks[f"oproj_{i}"])
                v.wait_ge(cXF[i % 2].sem, marks[f"xf_{i}"])
                reg = [4, 0, 2, 4][tc]
                if i >= 2:
                    v.wait_ge(cS.sem, marks[f"z1sq_{i-2}"])
                v.tensor_tensor(z1[i % 2][:, 0:512], PS[reg][:, :],
                                xf2[tc % 2][:, 0:512], OP.add)
                vchain(v, v.tensor_tensor(z1[i % 2][:, 512:1024], PS[reg + 1][:, :],
                                          xf2[tc % 2][:, 512:1024], OP.add))
                cV.inc(v.reduce_sum(stat[:, 0:1], z1[i % 2][:, :],
                                    axis=mybir.AxisListType.X))
                mark(f"z1_{i}", cV)

            @block.scalar
            def _(sc, i=i):
                sc.wait_ge(cV.sem, marks[f"z1_{i}"])
                cS.inc(sc.activation(sqs[:, :], z1[i % 2][:, :], AF.Square,
                                     accum_out=stat[:, 1:2]))
                mark(f"z1sq_{i}", cS)

            @block.vector
            def _(v, i=i):
                v.wait_ge(cS.sem, marks[f"z1sq_{i}"])
                ci = 1.0 / E
                vchain(v, v.tensor_scalar(stat[:, 2:3], stat[:, 0:1], ci, None,
                                          OP.mult))
                vchain(v, v.tensor_tensor(stat[:, 3:4], stat[:, 2:3],
                                          stat[:, 2:3], OP.mult))
                vchain(v, v.tensor_scalar(stat[:, 3:4], stat[:, 3:4], -1.0,
                                          LNEPS, OP.mult, OP.add))
                cV.inc(v.scalar_tensor_tensor(stat[:, 4:5], stat[:, 1:2], ci,
                                              stat[:, 3:4], OP.mult, OP.add))
                mark(f"var_{i}", cV)

            @block.scalar
            def _(sc, i=i):
                sc.wait_ge(cV.sem, marks[f"var_{i}"])
                cS.inc(sc.activation(stat[:, 5:6], stat[:, 4:5], AF.Sqrt))
                mark(f"sd_{i}", cS)

            @block.vector
            def _(v, i=i):
                v.wait_ge(cS.sem, marks[f"sd_{i}"])
                if i >= 3:
                    v.wait_ge(cYD[i % 2].sem, marks[f"ydma_{i-3}"])
                    v.wait_ge(cT.sem, marks[f"ytt_{i-3}"])
                vchain(v, v.reciprocal(stat[:, 6:7], stat[:, 5:6]))
                vchain(v, v.scalar_tensor_tensor(stat[:, 7:8], stat[:, 2:3],
                                                 -1.0, stat[:, 6:7],
                                                 OP.mult, OP.mult))
                cV.inc(v.tensor_scalar(ytr[i % 3][:, :], z1[i % 2][:, :],
                                       stat[:, 6:7], stat[:, 7:8], OP.mult, OP.add))
                mark(f"lnapply_{i}", cV)

            @block.sync
            def _(s, b=b, tc=tc, i=i):
                s.wait_ge(cV.sem, marks[f"lnapply_{i}"])
                s.wait_ge(cYD[i % 2].sem, cYD[i % 2].n)
                s.dma_start(out=YBUF.ap()[BLK * b + 128 * tc:BLK * b + 128 * tc + 128, :],
                            in_=ytr[i % 3][:, :]).then_inc(cYD[i % 2].sem, 16)
                cYD[i % 2].n += 16
                mark(f"ydma_{i}", cYD[i % 2])

    marks["y_done0"] = cYD[0].n
    marks["y_done1"] = cYD[1].n
    emit_ytt(13)
    emit_ytt(14)
    emit_ytt(15)

    if debug:
        @block.sync
        def _(s):
            s.wait_ge(cV.sem, cV.n)
            s.wait_ge(cS.sem, cS.n)
            s.wait_ge(cYD[0].sem, marks["y_done0"])
            s.wait_ge(cYD[1].sem, marks["y_done1"])
            s.wait_ge(cX.sem, cX.n)
            for nm, srcap in (("xT0", xT[0][:, :]), ("krot0", krot[0][:, :]),
                              ("kphi00", kphi[0][:, :]), ("w00", wnk[0][:, :]),
                              ("saug0", saug[0][:, :]), ("enq0", enq[:, :]),
                              ("g0", gsb[:, :]), ("oh0", oh[0][:, :]),
                              ("y0", YBUF.ap()[0:128, :])):
                s.dma_start(out=taps[nm], in_=srcap).then_inc(cX.sem, 16)
                cX.n += 16
            mark("tapmid", cX)

    @block.gpsimd
    def _(g):
        g.wait_ge(cT.sem, marks["oproj_15"])
        g.wait_ge(cW.sem, cW.n)
        for k in range(8):
            g.dma_start(out=WB[k][:], in_=WKF[128 * k:128 * (k + 1), :]).then_inc(cW.sem, 16)
            cW.n += 16
        mark("wkf", cW)
        g.wait_ge(cW.sem, cW.n)
        for k in range(8):
            g.dma_start(out=WC[k][:], in_=WVF[128 * k:128 * (k + 1), :]).then_inc(cW.sem, 16)
            cW.n += 16
        mark("wvf", cW)

    @block.sync
    def _(s):
        s.wait_ge(cYD[0].sem, marks["y_done0"])
        s.wait_ge(cYD[1].sem, marks["y_done1"])
        s.wait_ge(cX.sem, cX.n)
        s.dma_start(out=GIN.ap(), in_=ytr[15 % 3][127:128, :]).then_inc(cX.sem, 16)
        cX.n += 16
        mark("gin", cX)

    @block.gpsimd
    def _(g):
        g.wait_ge(cX.sem, marks["gin"])
        g.wait_ge(cYD[0].sem, marks["y_done0"])
        g.wait_ge(cYD[1].sem, marks["y_done1"])
        cC.inc(g.collective_compute(
            "AllGather", OP.bypass,
            replica_groups=[[0, 1, 2, 3, 4, 5, 6, 7]],
            ins=[GIN.ap().opt()], outs=[GOUT.ap().opt()]))
        mark("ag", cC)
        g.wait_ge(cC.sem, marks["ag"])
        g.wait_ge(cW.sem, cW.n)
        g.dma_start(out=agsb[:, :], in_=GOUT.ap()).then_inc(cW.sem, 16)
        cW.n += 16
        mark("agld", cW)

    PB = [1, 2, 3, 0]
    PRP = [0, 2, 4, 0]
    PWP = [4, 0, 2, 4]

    PTF = PTR[:, :].bitcast(F32)
    for bi, b in enumerate(PB):
        c0 = BLK * b
        prev = PB[bi - 1] if bi > 0 else None
        if bi == 1:
            @block.tensor
            def _(t):
                t.wait_ge(cW.sem, marks["agld"])
                t.wait_ge(cV.sem, marks["ytev2_15"])
                t.matmul(PTF[0:1, 0:512], hsel[:, :], agsb[:, 0:512],
                         start=True, stop=True)
                cT.inc(t.matmul(PTF[0:1, 512:1024], hsel[:, :], agsb[:, 512:1024],
                                start=True, stop=True))
                mark("halomm", cT)

            @block.scalar
            def _(sc):
                sc.wait_ge(cT.sem, marks["halomm"])
                sc.copy(hrow[0:1, 0:512], PTF[0:1, 0:512])
                cS.inc(sc.copy(hrow[0:1, 512:1024], PTF[0:1, 512:1024]))
                mark("hrow", cS)

            @block.tensor
            def _(t):
                t.wait_ge(cS.sem, marks["hrow"])
                last = None
                for k in range(8):
                    last = t.matmul(PTF[:, k:k + 1],
                                    hrow[0:1, 128 * k:128 * k + 128],
                                    ones11[0:1, 0:1], start=True, stop=True)
                cT.inc(last)
                mark("halot", cT)

            @block.vector
            def _(v):
                v.wait_ge(cT.sem, marks["halot"])
                cV.inc(v.tensor_copy(haloT[:, 0:8], PTF[:, 0:8]))
                mark("haloT", cV)
        for ec in range(8):
            @block.vector
            def _(v, b=b, ec=ec, c0=c0, prev=prev):
                v.wait_ge(cV.sem, marks[f"ytev2_{4*b+3}"])
                v.wait_ge(cV.sem, cV.n)
                if b == 0 and ec == 0:
                    v.wait_ge(cV.sem, marks["haloT"])
                if prev is not None and ec == 0:
                    v.wait_ge(cT.sem, marks[f"rmm_{4*prev+3}"])
                if c0 == 0:
                    v.tensor_tensor(dsb[:, 1:BLK], yT[ec][:, 1:BLK],
                                    yT[ec][:, 0:BLK - 1], OP.subtract)
                    vchain(v, v.tensor_tensor(dsb[:, 0:1], yT[ec][:, 0:1],
                                              haloT[:, ec:ec + 1], OP.subtract))
                    v.scalar_tensor_tensor(xk[ec][:, 1:BLK], dsb[:, 1:BLK],
                                           muk[:, ec:ec + 1], yT[ec][:, 0:BLK - 1],
                                           OP.mult, OP.add)
                    cV.inc(v.scalar_tensor_tensor(xk[ec][:, 0:1], dsb[:, 0:1],
                                                  muk[:, ec:ec + 1],
                                                  haloT[:, ec:ec + 1],
                                                  OP.mult, OP.add))
                    if ec == 7:
                        mark(f"mixk_{b}", cV)
                    v.scalar_tensor_tensor(xr[ec][:, 1:BLK], dsb[:, 1:BLK],
                                           mur[:, ec:ec + 1], yT[ec][:, 0:BLK - 1],
                                           OP.mult, OP.add)
                    cV.inc(v.scalar_tensor_tensor(xr[ec][:, 0:1], dsb[:, 0:1],
                                                  mur[:, ec:ec + 1],
                                                  haloT[:, ec:ec + 1],
                                                  OP.mult, OP.add))
                else:
                    vchain(v, v.tensor_tensor(dsb[:, :], yT[ec][:, c0:c0 + BLK],
                                              yT[ec][:, c0 - 1:c0 + BLK - 1],
                                              OP.subtract))
                    cV.inc(v.scalar_tensor_tensor(xk[ec][:, :], dsb[:, :],
                                                  muk[:, ec:ec + 1],
                                                  yT[ec][:, c0 - 1:c0 + BLK - 1],
                                                  OP.mult, OP.add))
                    if ec == 7:
                        mark(f"mixk_{b}", cV)
                    cV.inc(v.scalar_tensor_tensor(xr[ec][:, :], dsb[:, :],
                                                  mur[:, ec:ec + 1],
                                                  yT[ec][:, c0 - 1:c0 + BLK - 1],
                                                  OP.mult, OP.add))
                mark(f"mix_{8*b+ec}", cV)
        mark(f"mixb_{b}", cV)

        for dc in range(8):
            i = 8 * b + dc

            @block.tensor
            def _(t, b=b, dc=dc, i=i, bi=bi, prev=prev):
                t.wait_ge(cV.sem, marks[f"mixk_{b}"])
                if bi == 0 and dc == 0:
                    t.wait_ge(cW.sem, marks["wkf"])
                if dc < 2 and prev is not None:
                    t.wait_ge(cV.sem, marks[f"z2_{4*prev+2}"])
                if dc >= 2:
                    t.wait_ge(cV.sem, marks[f"kasq_{i-2}"])
                m = None
                for ec in range(8):
                    m = t.matmul(PS[2 + dc % 2][:, :],
                                 WB[ec][:, 128 * dc:128 * dc + 128],
                                 xk[ec][:, :], start=(ec == 0), stop=(ec == 7))
                cT.inc(m)
                mark(f"kamm_{i}", cT)

            @block.vector
            def _(v, dc=dc, i=i):
                v.wait_ge(cT.sem, marks[f"kamm_{i}"])
                vchain(v, v.tensor_scalar(kr2[dc % 2][:, :], PS[2 + dc % 2][:, :],
                                          0.0, None, OP.max))
                cV.inc(v.tensor_tensor(kact[dc][:, :], kr2[dc % 2][:, :],
                                       kr2[dc % 2][:, :], OP.mult))
                mark(f"kasq_{i}", cV)

        for tc in range(4):
            i = 4 * b + tc
            pr = PRP[tc]
            pw = PWP[tc]

            @block.tensor
            def _(t, b=b, tc=tc, i=i, pr=pr, bi=bi, prev=prev):
                t.wait_ge(cV.sem, marks[f"mixb_{b}"])
                if bi == 0 and tc == 0:
                    t.wait_ge(cW.sem, marks["wr"])
                if tc == 0:
                    if prev is not None:
                        t.wait_ge(cS.sem, marks[f"rsig_{4*prev+3}"])
                elif tc == 1:
                    t.wait_ge(cV.sem, marks[f"kasq_{8*b+7}"])
                elif tc == 2:
                    t.wait_ge(cV.sem, marks[f"z2_{4*b+0}"])
                else:
                    t.wait_ge(cV.sem, marks[f"z2_{4*b+1}"])
                m = None
                for ec in range(8):
                    t.matmul(PS[pr][:, :], xr[ec][:, 128 * tc:128 * tc + 128],
                             WA[ec][:, 0:512], start=(ec == 0), stop=(ec == 7))
                for ec in range(8):
                    m = t.matmul(PS[pr + 1][:, :], xr[ec][:, 128 * tc:128 * tc + 128],
                                 WA[ec][:, 512:1024], start=(ec == 0), stop=(ec == 7))
                cT.inc(m)
                mark(f"rmm_{i}", cT)

            @block.scalar
            def _(sc, b=b, tc=tc, i=i, pr=pr, bi=bi, prev=prev):
                sc.wait_ge(cT.sem, marks[f"rmm_{i}"])
                if tc >= 2:
                    sc.wait_ge(cV.sem, marks[f"z2_{4*b+tc-2}"])
                elif prev is not None:
                    sc.wait_ge(cV.sem, marks[f"z2_{4*prev+2+tc}"])
                sc.activation(rsb[tc % 2][:, 0:512], PS[pr][:, :], AF.Sigmoid)
                cS.inc(sc.activation(rsb[tc % 2][:, 512:1024], PS[pr + 1][:, :],
                                     AF.Sigmoid))
                mark(f"rsig_{i}", cS)

            @block.tensor
            def _(t, b=b, tc=tc, i=i, pw=pw, bi=bi, prev=prev):
                t.wait_ge(cV.sem, marks[f"kasq_{8*b+7}"])
                if bi == 0 and tc == 0:
                    t.wait_ge(cW.sem, marks["wvf"])
                if tc == 0:
                    if prev is not None:
                        t.wait_ge(cV.sem, marks[f"z2_{4*prev+3}"])
                elif tc == 1:
                    t.wait_ge(cS.sem, marks[f"rsig_{4*b+0}"])
                elif tc == 2:
                    t.wait_ge(cS.sem, marks[f"rsig_{4*b+1}"])
                else:
                    t.wait_ge(cS.sem, marks[f"rsig_{4*b+2}"])
                m = None
                for dc in range(8):
                    t.matmul(PS[pw][:, :], kact[dc][:, 128 * tc:128 * tc + 128],
                             WC[dc][:, 0:512], start=(dc == 0), stop=(dc == 7))
                for dc in range(8):
                    m = t.matmul(PS[pw + 1][:, :], kact[dc][:, 128 * tc:128 * tc + 128],
                                 WC[dc][:, 512:1024], start=(dc == 0), stop=(dc == 7))
                cT.inc(m)
                mark(f"wvmm_{i}", cT)

            @block.sync
            def _(s, b=b, tc=tc, i=i, bi=bi, prev=prev):
                if tc >= 2:
                    s.wait_ge(cV.sem, marks[f"z2_{4*b+tc-2}"])
                elif prev is not None:
                    s.wait_ge(cV.sem, marks[f"z2_{4*prev+2+tc}"])
                s.wait_ge(cY2[i % 2].sem, cY2[i % 2].n)
                s.dma_start(out=y2[i % 2][:, :],
                            in_=YBUF.ap()[BLK * b + 128 * tc:BLK * b + 128 * tc + 128, :]
                            ).then_inc(cY2[i % 2].sem, 16)
                cY2[i % 2].n += 16
                mark(f"y2_{i}", cY2[i % 2])

            @block.vector
            def _(v, tc=tc, i=i, pw=pw):
                v.wait_ge(cT.sem, marks[f"wvmm_{i}"])
                v.wait_ge(cS.sem, marks[f"rsig_{i}"])
                v.wait_ge(cY2[i % 2].sem, marks[f"y2_{i}"])
                v.wait_ge(cX.sem, marks["gin"])
                if i == 4:
                    v.wait_ge(cT.sem, marks["ytt_15"])
                elif i == 5:
                    v.wait_ge(cT.sem, marks["ytt_13"])
                v.tensor_tensor(tmul[i % 2][:, 0:512], PS[pw][:, :],
                                rsb[tc % 2][:, 0:512], OP.mult)
                vchain(v, v.tensor_tensor(tmul[i % 2][:, 512:1024], PS[pw + 1][:, :],
                                          rsb[tc % 2][:, 512:1024], OP.mult))
                vchain(v, v.tensor_tensor(z2[i % 2][:, :], tmul[i % 2][:, :],
                                          y2[i % 2][:, :], OP.add))
                cV.inc(v.reduce_sum(stat[:, 0:1], z2[i % 2][:, :],
                                    axis=mybir.AxisListType.X))
                mark(f"z2_{i}", cV)

            @block.scalar
            def _(sc, i=i):
                sc.wait_ge(cV.sem, marks[f"z2_{i}"])
                cS.inc(sc.activation(sqs[:, :], z2[i % 2][:, :], AF.Square,
                                     accum_out=stat[:, 1:2]))
                mark(f"z2sq_{i}", cS)

            @block.vector
            def _(v, i=i):
                v.wait_ge(cS.sem, marks[f"z2sq_{i}"])
                ci = 1.0 / E
                vchain(v, v.tensor_scalar(stat[:, 2:3], stat[:, 0:1], ci, None,
                                          OP.mult))
                vchain(v, v.tensor_tensor(stat[:, 3:4], stat[:, 2:3],
                                          stat[:, 2:3], OP.mult))
                vchain(v, v.tensor_scalar(stat[:, 3:4], stat[:, 3:4], -1.0,
                                          LNEPS, OP.mult, OP.add))
                cV.inc(v.scalar_tensor_tensor(stat[:, 4:5], stat[:, 1:2], ci,
                                              stat[:, 3:4], OP.mult, OP.add))
                mark(f"var2_{i}", cV)

            @block.scalar
            def _(sc, i=i):
                sc.wait_ge(cV.sem, marks[f"var2_{i}"])
                cS.inc(sc.activation(stat[:, 5:6], stat[:, 4:5], AF.Sqrt))
                mark(f"sd2_{i}", cS)

            @block.vector
            def _(v, i=i, b=b, tc=tc, prev=prev):
                v.wait_ge(cS.sem, marks[f"sd2_{i}"])
                if tc >= 2:
                    v.wait_ge(cOD[i % 2].sem, marks[f"odma_{4*b+tc-2}"])
                elif prev is not None:
                    v.wait_ge(cOD[(4*prev+2+tc) % 2].sem, marks[f"odma_{4*prev+2+tc}"])
                vchain(v, v.reciprocal(stat[:, 6:7], stat[:, 5:6]))
                vchain(v, v.scalar_tensor_tensor(stat[:, 7:8], stat[:, 2:3],
                                                 -1.0, stat[:, 6:7],
                                                 OP.mult, OP.mult))
                cV.inc(v.tensor_scalar(osb[i % 2][:, :], z2[i % 2][:, :],
                                       stat[:, 6:7], stat[:, 7:8], OP.mult, OP.add))
                mark(f"lnout_{i}", cV)

            @block.sync
            def _(s, b=b, tc=tc, i=i):
                s.wait_ge(cV.sem, marks[f"lnout_{i}"])
                s.wait_ge(cOD[i % 2].sem, cOD[i % 2].n)
                s.dma_start(out=OUT[BLK * b + 128 * tc:BLK * b + 128 * tc + 128, :],
                            in_=osb[i % 2][:, :]).then_inc(cOD[i % 2].sem, 16)
                cOD[i % 2].n += 16
                mark(f"odma_{i}", cOD[i % 2])

    if debug:
        @block.sync
        def _(s):
            s.wait_ge(cOD[0].sem, cOD[0].n)
            s.wait_ge(cOD[1].sem, cOD[1].n)
            s.wait_ge(cV.sem, cV.n)
            s.wait_ge(cS.sem, cS.n)
            s.wait_ge(cX.sem, cX.n)
            for nm, srcap in (("mt", mt[:, :]), ("xr0", xr[0][:, :]),
                              ("kact0", kact[0][:, :]), ("z20", z2[0][:, :])):
                s.dma_start(out=taps[nm], in_=srcap).then_inc(cX.sem, 16)
                cX.n += 16

    @block.sync
    def _(s):
        s.wait_ge(cOD[0].sem, cOD[0].n)
        s.wait_ge(cOD[1].sem, cOD[1].n)
        s.wait_ge(cX.sem, cX.n)

    return nc


def _in_maps(inputs):
    x = np.ascontiguousarray(np.asarray(inputs["L0_attn_x"], dtype=np.float32))
    w_qkv = np.asarray(inputs["L0_attn_w_qkv"], dtype=np.float32)
    w_out = np.asarray(inputs["L0_attn_w_out"], dtype=np.float32)
    signs = np.asarray(inputs["L0_attn_sorf_signs"], dtype=np.float32)
    scales = np.asarray(inputs["L0_attn_sorf_scales"], dtype=np.float32)
    W_r = np.asarray(inputs["L0_ffn_W_r"], dtype=np.float32)
    W_k = np.asarray(inputs["L0_ffn_W_k"], dtype=np.float32)
    W_v = np.asarray(inputs["L0_ffn_W_v"], dtype=np.float32)
    mu_r = np.asarray(inputs["L0_ffn_mu_r"], dtype=np.float32)
    mu_k = np.asarray(inputs["L0_ffn_mu_k"], dtype=np.float32)

    consts = _host_consts()
    shared = {
        "wq": np.ascontiguousarray(w_qkv[0:E, :].T),
        "wk": np.ascontiguousarray(w_qkv[E:2 * E, :].T),
        "wv": np.ascontiguousarray(w_qkv[2 * E:3 * E, :].T),
        "wo": np.ascontiguousarray(w_out.T),
        "wr": np.ascontiguousarray(W_r.T),
        "wkf": np.ascontiguousarray(W_k.T),
        "wvf": np.ascontiguousarray(W_v.T),
        "mur": np.ascontiguousarray(mu_r.reshape(8, 128).T),
        "muk": np.ascontiguousarray(mu_k.reshape(8, 128).T),
        "scl": np.ascontiguousarray(np.hstack([scales.reshape(64, 64)] * 2)),
        "h64": consts["h64"], "i64rep": consts["i64rep"],
        "idnb": consts["idnb"], "idnf": consts["idnf"],
        "selnq": consts["selnq"], "selg": consts["selg"],
    }
    for s in range(3):
        shared[f"sg{s+1}"] = np.ascontiguousarray(
            np.hstack([signs[:, :, s, :].reshape(64, 64)] * 2))

    in_maps = []
    for i in range(NCORES):
        c, j = divmod(i, 4)
        t0 = j * T
        Ct, St = _rope_tables(t0)
        hs = np.zeros((8, 1), dtype=np.float32)
        if j > 0:
            hs[i - 1, 0] = 1.0
        m = dict(shared)
        m["x"] = np.ascontiguousarray(x[c, t0:t0 + T, :])
        m["ropec"] = Ct
        m["ropes"] = St
        m["hsel"] = hs
        in_maps.append(m)
    return in_maps


def kernel(**inputs):
    in_maps = _in_maps(inputs)
    nc = build_nc(debug=False)
    res = run_bass_kernel_spmd(nc, in_maps, core_ids=list(range(NCORES)))
    out = np.empty((2, 8192, E), dtype=np.float32)
    for i in range(NCORES):
        c, j = divmod(i, 4)
        out[c, j * T:(j + 1) * T, :] = res.results[i]["out"]
    return out


# version 13
# speedup vs baseline: 1.0324x; 1.0324x over previous
import math
from contextlib import ExitStack

import numpy as np
import ml_dtypes

import concourse.bass as bass
import concourse.mybir as mybir
from concourse.bass_utils import run_bass_kernel_spmd

F32 = mybir.dt.float32
BF16 = mybir.dt.bfloat16
AF = mybir.ActivationFunctionType
OP = mybir.AluOpType
BFNP = ml_dtypes.bfloat16

NCORES = 8
T = 2048
NB = 4
BLK = 512
NTC = 16
E = 1024
H = 16
D = 64
MF = 256
MEPS = 256.0 * 1e-6
LNEPS = 1e-5
SWAP32 = [i ^ 1 for i in range(32)]


def _hadamard(n):
    m = np.array([[1.0]], dtype=np.float32)
    while m.shape[0] < n:
        m = np.block([[m, m], [m, -m]])
    return m


def _host_consts():
    c = {}
    H64 = _hadamard(64)
    c["h64"] = np.vstack([H64, H64]).astype(BFNP)
    I64 = np.eye(64, dtype=np.float32)
    rep = np.hstack([I64] * 8)
    c["i64rep"] = np.vstack([rep, rep]).astype(np.float32)
    c["idnb"] = np.eye(128, dtype=np.float32).astype(BFNP)
    c["idnf"] = np.eye(128, dtype=np.float32)
    selnq = np.zeros((128, 128), dtype=np.float32)
    for dc in range(8):
        selnq[0:64, 16 * dc + 2 * dc] = 1.0
        selnq[64:128, 16 * dc + 2 * dc + 1] = 1.0
    c["selnq"] = selnq.astype(BFNP)
    selg = np.zeros((16, 1024), dtype=np.float32)
    for p in range(8):
        selg[2 * p, 128 * p:128 * p + 64] = 1.0
        selg[2 * p + 1, 128 * p + 64:128 * p + 128] = 1.0
    c["selg"] = selg.astype(BFNP)
    return c


def _rope_tables(t0):
    d = np.arange(D)
    freqs = 10000.0 ** (-(2 * (d // 2)) / D)
    g = np.arange(t0, t0 + T, dtype=np.float64)
    ang = np.where(g[None, :] > 0, (g[None, :] - 1) * freqs[:, None], 0.0)
    Ct = np.where(g[None, :] > 0, np.cos(ang), 1.0)
    sgn = np.where(d % 2 == 0, -1.0, 1.0)[:, None]
    St = np.where(g[None, :] > 0, sgn * np.sin(ang), 0.0)
    return (np.vstack([Ct, Ct]).astype(BFNP), np.vstack([St, St]).astype(BFNP))


class _Ctr:
    def __init__(self, ctx, nc, name):
        self.sem = ctx.enter_context(nc.semaphore(name))
        self.n = 0

    def inc(self, instr, k=1):
        instr.then_inc(self.sem, k)
        self.n += k
        return self.n


def build_nc(debug=False):
    nc = bass.Bass("TRN2", target_bir_lowering=False)

    def din(name, shape, dtype=F32):
        return nc.dram_tensor(name, shape, dtype, kind="ExternalInput").ap()

    X = din("x", [T, E])
    WQ = din("wq", [E, E]); WK = din("wk", [E, E]); WV = din("wv", [E, E])
    WO = din("wo", [E, E])
    WR = din("wr", [E, E]); WKF = din("wkf", [E, E]); WVF = din("wvf", [E, E])
    MUR = din("mur", [128, 8]); MUK = din("muk", [128, 8])
    SG = [din(f"sg{s}", [64, 128]) for s in (1, 2, 3)]
    SCL = din("scl", [64, 128])
    H64 = din("h64", [128, 64], BF16)
    I64R = din("i64rep", [128, 512])
    IDNB = din("idnb", [128, 128], BF16)
    IDNF = din("idnf", [128, 128])
    SELNQ = din("selnq", [128, 128], BF16)
    SELG = din("selg", [16, 1024], BF16)
    ROPEC = din("ropec", [128, T], BF16)
    ROPES = din("ropes", [128, T], BF16)
    HSEL = din("hsel", [8, 1])
    OUT = nc.dram_tensor("out", [T, E], F32, kind="ExternalOutput").ap()

    YBUF = nc.dram_tensor("ybuf", [T, E], F32, kind="Internal")
    SBIN = nc.dram_tensor("sbin", [H * 128, MF], BF16, kind="Internal")
    SBOUT = nc.dram_tensor("sbout", [H * 128, MF], BF16, kind="Internal")
    GIN = nc.dram_tensor("gin", [1, E], F32, kind="Internal")
    GOUT = nc.dram_tensor("gout", [8, E], F32, kind="Internal")

    taps = {}
    if debug:
        def tap(name, shape, dtype=F32):
            taps[name] = nc.dram_tensor("tap_" + name, shape, dtype,
                                        kind="ExternalOutput").ap()
        tap("mt", [128, 4096], BF16); tap("xT0", [128, T], BF16)
        tap("krot0", [128, BLK], BF16); tap("kphi00", [128, 4096], BF16)
        tap("w00", [128, 16]); tap("saug0", [128, 1040], BF16)
        tap("enq0", [16, BLK]); tap("g0", [16, BLK], BF16)
        tap("oh0", [128, BLK], BF16); tap("y0", [128, E])
        tap("xr0", [128, BLK], BF16); tap("kact0", [128, BLK], BF16)
        tap("z20", [128, E])

    ctx = ExitStack()

    def sb(name, shape, dtype):
        return ctx.enter_context(nc.sbuf_tensor(name, shape, dtype))

    def psa(name, shape, dtype=F32):
        return ctx.enter_context(nc.psum_tensor(name, shape, dtype))

    XTB = sb("xtb_s", [128, 8 * T], BF16)
    xT = [XTB[:, T * k:T * (k + 1)] for k in range(8)]
    yT = xT
    xb = [sb(f"xb{t}", [128, E], BF16) for t in range(8)]
    WA = [sb(f"wA{k}", [128, E], BF16) for k in range(8)]
    WB = [sb(f"wB{k}", [128, E], BF16) for k in range(8)]
    mt = sb("mt_s", [128, 4096], BF16)
    ropec = sb("ropec_s", [128, T], BF16)
    ropes = sb("ropes_s", [128, T], BF16)
    idnb = sb("idnb_s", [128, 128], BF16)
    idnf = sb("idnf_s", [128, 128], F32)
    selnq = sb("selnq_s", [128, 128], BF16)
    selg = sb("selg_s", [16, 1024], BF16)
    h64 = sb("h64_s", [128, 64], BF16)
    i64r = sb("i64r", [128, 512], F32)
    sg = [sb(f"sgs{s}", [64, 128], F32) for s in range(3)]
    scl = sb("scl_s", [64, 128], F32)
    sgT = [sb(f"sgT{s}", [128, 64], F32) for s in range(3)]
    gT = sb("gT", [128, 64], F32)
    r0 = sb("r0", [128, 512], BF16)
    rt = sb("rt", [128, 512], BF16)
    mur = sb("mur_s", [128, 8], F32)
    muk = sb("muk_s", [128, 8], F32)
    hsel = sb("hsel_s", [8, 1], F32)

    ksb = [sb(f"ksb{i}", [128, BLK], F32) for i in range(2)]
    krot = [sb(f"krot{d}", [128, BLK], BF16) for d in range(8)]
    xr = krot
    kr2 = [sb(f"kr2{i}", [128, BLK], BF16) for i in range(2)]
    nkc = sb("nkc", [16, BLK], F32)
    wnk = [sb(f"wnk{t}", [128, 16], F32) for t in range(4)]
    AR1 = sb("AR1", [128, 16384], BF16)
    kphi = [AR1[:, 4096 * t:4096 * (t + 1)] for t in range(4)]
    qexp = [AR1[:, 512 * i:512 * (i + 1)] for i in range(32)]
    xk = [AR1[:, 512 * k:512 * (k + 1)] for k in range(8)]
    kact = [AR1[:, 4096 + 512 * k:4096 + 512 * (k + 1)] for k in range(8)]
    WC = [AR1[:, 8192 + 1024 * k:8192 + 1024 * (k + 1)] for k in range(8)]
    SSA = sb("SSA", [128, 4096], F32)
    Ssb = [SSA[:, MF * h:MF * (h + 1)] for h in range(H)]
    z1 = [SSA[:, 1024 * i:1024 * (i + 1)] for i in range(2)]
    y2 = [SSA[:, 1024 * i:1024 * (i + 1)] for i in range(2)]
    sqs = SSA[:, 2048:3072]
    SSAB16 = SSA[:, :].bitcast(BF16)
    VAA = sb("VAA", [128, 4160], BF16)
    vaug = [VAA[:, 1040 * t:1040 * (t + 1)] for t in range(4)]
    VAF = VAA[:, :].bitcast(F32)
    xf2 = [VAF[:, 1024 * i:1024 * (i + 1)] for i in range(2)]
    osb = [VAF[:, 1024 * i:1024 * (i + 1)] for i in range(2)]
    saug = [sb(f"saug{m}", [128, 1040], BF16) for m in range(2)]
    skc = [sb(f"skc{m}", [128, 16], BF16) for m in range(2)]
    GAR = sb("GAR", [128, 2304], F32)
    GARB = GAR[:, :].bitcast(BF16)
    enq = GAR[0:16, 0:512]
    gch = [GAR[0:16, 512 + 512 * i:1024 + 512 * i] for i in range(2)]
    gsb = GARB[0:16, 3072:3584]
    gx = [GARB[:, 3584 + 512 * i:4096 + 512 * i] for i in range(2)]
    tmul = [GAR[:, 1024 * i:1024 * (i + 1)] for i in range(2)]
    dsb = GARB[:, 4096:4608]
    OHA = sb("OHA", [128, 4096], BF16)
    oh = [OHA[:, 512 * p:512 * (p + 1)] for p in range(8)]
    rsb = [OHA[:, 1024 * i:1024 * (i + 1)] for i in range(2)]
    OHAF = OHA[:, :].bitcast(F32)
    agsb = OHAF[0:8, 1024:2048]
    hrow = SSAB16[0:1, 6144:7168]
    stat = sb("stat", [128, 16], F32)
    ytr = [sb(f"ytr{i}", [128, E], F32) for i in range(2)]
    z2 = ytr[0:2]
    ytr.append(SSA[:, 3072:4096])
    haloT = sb("haloT", [128, 8], BF16)
    ones11 = sb("ones11", [1, 1], BF16)

    PS = [psa(f"PS{i}", [128, 512], F32) for i in range(6)]
    PTR = psa("PTR", [128, 2048], BF16)

    cW = _Ctr(ctx, nc, "cW")
    cX = _Ctr(ctx, nc, "cX")
    cT = _Ctr(ctx, nc, "cT")
    cV = _Ctr(ctx, nc, "cV")
    cS = _Ctr(ctx, nc, "cS")
    cG = _Ctr(ctx, nc, "cG")
    cC = _Ctr(ctx, nc, "cC")
    cXF = [_Ctr(ctx, nc, f"cXF{i}") for i in range(2)]
    cYD = [_Ctr(ctx, nc, f"cYD{i}") for i in range(2)]
    cY2 = [_Ctr(ctx, nc, f"cY2{i}") for i in range(2)]
    cOD = [_Ctr(ctx, nc, f"cOD{i}") for i in range(2)]

    block = ctx.enter_context(nc.Block())
    marks = {}

    def mark(name, ctr):
        marks[name] = ctr.n

    def vchain(v, instr):
        cV.inc(instr)
        v.wait_ge(cV.sem, cV.n)

    @block.gpsimd
    def _(g):
        for s, dst in ((IDNB, idnb), (IDNF, idnf), (SELNQ, selnq),
                       (SELG, selg), (H64, h64), (I64R, i64r),
                       (ROPEC, ropec), (ROPES, ropes),
                       (MUR, mur), (MUK, muk), (HSEL, hsel),
                       (SG[0], sg[0]), (SG[1], sg[1]), (SG[2], sg[2]), (SCL, scl)):
            g.dma_start(out=dst[:], in_=s[:]).then_inc(cW.sem, 16)
            cW.n += 16
        mark("consts", cW)
        g.wait_ge(cW.sem, cW.n)
        for t in range(8):
            g.dma_start(out=xb[t][:], in_=X[128 * t:128 * (t + 1), :]).then_inc(cW.sem, 16)
            cW.n += 16
        mark("xbg_0", cW)
        g.wait_ge(cW.sem, cW.n)
        for k in range(8):
            g.dma_start(out=WA[k][:], in_=WK[128 * k:128 * (k + 1), :]).then_inc(cW.sem, 16)
            cW.n += 16
        mark("wk", cW)
        g.wait_ge(cW.sem, cW.n)
        for k in range(8):
            g.dma_start(out=WB[k][:], in_=WV[128 * k:128 * (k + 1), :]).then_inc(cW.sem, 16)
            cW.n += 16
        mark("wv", cW)
        g.memset(ones11[:], 1.0)
        for m in range(2):
            g.memset(skc[m][:], 0.0)
        for h in range(H):
            g.memset(Ssb[h][64:128, :], 0.0)
        g.memset(haloT[:], 0.0).then_inc(cG.sem, 1)
        cG.n += 1
        mark("memsets", cG)

    @block.tensor
    def _(t):
        t.wait_ge(cW.sem, marks["consts"])
        t.wait_ge(cG.sem, marks["memsets"])
        for s in range(3):
            t.transpose(PS[s][0:128, 0:64], sg[s][:, :], idnf[0:64, 0:64])
        cT.inc(t.transpose(PS[3][0:128, 0:64], scl[:, :], idnf[0:64, 0:64]))
        mark("sgT_ps", cT)

    @block.vector
    def _(v):
        v.wait_ge(cT.sem, marks["sgT_ps"])
        for s in range(3):
            v.tensor_copy(sgT[s][:, :], PS[s][0:128, 0:64])
        cV.inc(v.tensor_scalar(gT[:, :], PS[3][0:128, 0:64], 1.0 / 4096.0,
                               None, OP.mult))
        mark("sgT", cV)

    for b in range(8):
        @block.vector
        def _(v, b=b):
            v.wait_ge(cV.sem, marks["sgT"])
            if b >= 1:
                v.wait_ge(cT.sem, marks[f"mb_mm2_{b-1}"])
            gbc = gT[:, 8 * b:8 * b + 8].unsqueeze(2).broadcast_to([128, 8, 64])
            cV.inc(v.tensor_tensor(r0[:, :], i64r[:, :], gbc, OP.mult))
            mark(f"mb_r0_{b}", cV)

        @block.tensor
        def _(t, b=b):
            t.wait_ge(cV.sem, marks[f"mb_r0_{b}"])
            t.matmul(PS[4][0:64, :], h64[0:64, :], r0[0:64, :], start=True, stop=True)
            cT.inc(t.matmul(PS[4][64:128, :], h64[64:128, :], r0[64:128, :],
                            start=True, stop=True, tile_position=(64, 64)))
            mark(f"mb_mm0_{b}", cT)

        @block.vector
        def _(v, b=b):
            v.wait_ge(cT.sem, marks[f"mb_mm0_{b}"])
            sbc = sgT[2][:, 8 * b:8 * b + 8].unsqueeze(2).broadcast_to([128, 8, 64])
            cV.inc(v.tensor_tensor(rt[:, :], PS[4][:, :], sbc, OP.mult))
            mark(f"mb_r3_{b}", cV)

        @block.tensor
        def _(t, b=b):
            t.wait_ge(cV.sem, marks[f"mb_r3_{b}"])
            t.matmul(PS[5][0:64, :], h64[0:64, :], rt[0:64, :], start=True, stop=True)
            cT.inc(t.matmul(PS[5][64:128, :], h64[64:128, :], rt[64:128, :],
                            start=True, stop=True, tile_position=(64, 64)))
            mark(f"mb_mm1_{b}", cT)

        @block.vector
        def _(v, b=b):
            v.wait_ge(cT.sem, marks[f"mb_mm1_{b}"])
            sbc = sgT[1][:, 8 * b:8 * b + 8].unsqueeze(2).broadcast_to([128, 8, 64])
            cV.inc(v.tensor_tensor(rt[:, :], PS[5][:, :], sbc, OP.mult))
            mark(f"mb_r2_{b}", cV)

        @block.tensor
        def _(t, b=b):
            t.wait_ge(cV.sem, marks[f"mb_r2_{b}"])
            t.matmul(PS[4][0:64, :], h64[0:64, :], rt[0:64, :], start=True, stop=True)
            cT.inc(t.matmul(PS[4][64:128, :], h64[64:128, :], rt[64:128, :],
                            start=True, stop=True, tile_position=(64, 64)))
            mark(f"mb_mm2_{b}", cT)

        @block.vector
        def _(v, b=b):
            v.wait_ge(cT.sem, marks[f"mb_mm2_{b}"])
            sbc = sgT[0][:, 8 * b:8 * b + 8].unsqueeze(2).broadcast_to([128, 8, 64])
            cV.inc(v.tensor_tensor(mt[:, 512 * b:512 * b + 512], PS[4][:, :],
                                   sbc, OP.mult))
            mark(f"mb_done_{b}", cV)
    mark("mt_done", cV)

    for grp in range(2):
        if grp == 1:
            @block.gpsimd
            def _(g):
                g.wait_ge(cT.sem, marks["xtr_7"])
                g.wait_ge(cW.sem, cW.n)
                for t8 in range(8):
                    g.dma_start(out=xb[t8][:],
                                in_=X[128 * (8 + t8):128 * (9 + t8), :]
                                ).then_inc(cW.sem, 16)
                    cW.n += 16
                mark("xbg_1", cW)
        for kc in range(8):
            idx = grp * 8 + kc
            half = idx % 2

            @block.tensor
            def _(t, grp=grp, kc=kc, idx=idx, half=half):
                if kc == 0:
                    t.wait_ge(cW.sem, marks[f"xbg_{grp}"])
                if idx >= 2:
                    t.wait_ge(cV.sem, marks[f"xtev_{idx-2}"])
                last = None
                for tc8 in range(8):
                    last = t.transpose(PTR[:, 1024 * half + 128 * tc8:
                                           1024 * half + 128 * tc8 + 128],
                                       xb[tc8][:, 128 * kc:128 * kc + 128],
                                       idnb[:, :])
                cT.inc(last)
                mark(f"xtr_{idx}", cT)

            @block.vector
            def _(v, kc=kc, grp=grp, idx=idx, half=half):
                v.wait_ge(cT.sem, marks[f"xtr_{idx}"])
                cV.inc(v.tensor_copy(xT[kc][:, 1024 * grp:1024 * grp + 1024],
                                     PTR[:, 1024 * half:1024 * half + 1024]))
                mark(f"xtev_{idx}", cV)
    mark("xT_done", cV)

    def emit_qk_side(b, base, c0):
        kside = (base == 0)

        def emit_phi_round(dc, r):
            i = base + 8 * b + dc

            @block.tensor
            def _(t, dc=dc, i=i, r=r, b=b):
                t.wait_ge(cG.sem, marks[f"grot_{i}"])
                if not kside and b > 0 and dc == 0 and r == 0:
                    t.wait_ge(cV.sem, marks[f"z1_{4*(b-1)+2}"])
                pr = 2 * i + r - 1
                if pr >= 2 * base:
                    t.wait_ge(cS.sem, marks[f"per_{pr}"])
                m = None
                for par in range(2):
                    h = 2 * dc + par
                    lo = 64 * par
                    if kside:
                        for tcs in range(2):
                            tc = 2 * r + tcs
                            m = t.matmul(PS[2 + par][:, 256 * tcs:256 * tcs + 256],
                                         krot[dc][lo:lo + 64, 128 * tc:128 * tc + 128],
                                         mt[lo:lo + 64, 256 * h:256 * h + 256],
                                         start=True, stop=True)
                    else:
                        m = t.matmul(PS[2 + par][:, :],
                                     mt[lo:lo + 64, 256 * h + 128 * r:
                                        256 * h + 128 * r + 128],
                                     krot[dc][lo:lo + 64, :],
                                     start=True, stop=True)
                cT.inc(m)
                mark(f"phr_{2*i+r}", cT)

            @block.scalar
            def _(sc, dc=dc, i=i, r=r):
                sc.wait_ge(cT.sem, marks[f"phr_{2*i+r}"])
                last = None
                for par in range(2):
                    h = 2 * dc + par
                    if kside:
                        dv = AR1[:, 4096 * 2 * r + 256 * h:
                                 4096 * 2 * r + 256 * h + 4352]
                        dv = dv.rearrange("p (a c) -> p a c", a=17)[:, 0:17:16, :]
                        last = sc.activation(
                            dv,
                            PS[2 + par][:, :].rearrange("p (a c) -> p a c", a=2),
                            AF.Exp)
                    else:
                        qi = 2 * (2 * dc + par) + r
                        last = sc.activation(qexp[qi][:, :], PS[2 + par][:, :],
                                             AF.Exp)
                cS.inc(last)
                mark(f"per_{2*i+r}", cS)
            if r == 1:
                marks[f"pe_{i}"] = marks[f"per_{2*i+1}"]

        pending = []
        for dc in range(8):
            i = base + 8 * b + dc

            @block.tensor
            def _(t, b=b, dc=dc, c0=c0, i=i):
                if kside and b == 0 and dc == 0:
                    t.wait_ge(cW.sem, marks["wk"])
                    t.wait_ge(cV.sem, marks["xT_done"])
                if not kside and b == 0 and dc == 0:
                    t.wait_ge(cW.sem, marks["wq"])
                if dc < 2:
                    if b > 0:
                        t.wait_ge(cS.sem, marks[f"vev_{4*(b-1)+3}"] if kside
                                  else marks[f"pe_{base + 8*(b-1) + 7}"])
                        if not kside:
                            t.wait_ge(cV.sem, marks[f"z1_{4*(b-1)+1}"])
                    elif not kside:
                        t.wait_ge(cS.sem, marks["vev_15"])
                        t.wait_ge(cV.sem, marks["sfold_63"])
                if (b, dc) > (0, 1):
                    t.wait_ge(cV.sem, marks[f"rope_{i-2}"])
                m = None
                for kc in range(8):
                    m = t.matmul(PS[dc % 2][:, :], WA[kc][:, 128 * dc:128 * dc + 128],
                                 xT[kc][:, c0:c0 + BLK],
                                 start=(kc == 0), stop=(kc == 7))
                cT.inc(m)
                mark(f"kmm_{i}", cT)

            if pending:
                emit_phi_round(*pending.pop(0))

            @block.vector
            def _(v, dc=dc, c0=c0, i=i):
                v.wait_ge(cT.sem, marks[f"kmm_{i}"])
                if i >= 2:
                    v.wait_ge(cG.sem, marks[f"grot_{i-2}"])
                vchain(v, v.stream_shuffle(ksb[dc % 2][:, :], PS[dc % 2][:, :],
                                           SWAP32))
                cV.inc(v.tensor_tensor(kr2[dc % 2][:, :], PS[dc % 2][:, :],
                                       ropec[:, c0:c0 + BLK], OP.mult))
                mark(f"rope_{i}", cV)

            @block.gpsimd
            def _(g, dc=dc, c0=c0, i=i):
                g.wait_ge(cV.sem, marks[f"rope_{i}"])
                cG.inc(g.tensor_tensor(krot[dc][:, :], ksb[dc % 2][:, :],
                                       ropes[:, c0:c0 + BLK], OP.mult))
                g.wait_ge(cG.sem, cG.n)
                cG.inc(g.tensor_tensor(krot[dc][:, :], krot[dc][:, :],
                                       kr2[dc % 2][:, :], OP.add))
                mark(f"grot_{i}", cG)

            if not kside and dc in (2, 4) and lnq:
                lnq.pop(0)()

            if len(pending) > 2:
                emit_phi_round(*pending.pop(0))
            pending.append((dc, 0))
            pending.append((dc, 1))
        while pending:
            emit_phi_round(*pending.pop(0))

        for dc in range(8):
            i = base + 8 * b + dc

            @block.vector
            def _(v, dc=dc, i=i):
                if dc == 0:
                    v.wait_ge(cV.sem, cV.n)
                if dc < 2:
                    v.wait_ge(cG.sem, marks[f"grot_{i - dc + 7}"])
                if dc >= 2:
                    v.wait_ge(cT.sem, marks[f"nmm_{i-2}"])
                cV.inc(v.tensor_tensor(kr2[dc % 2][:, :], krot[dc][:, :],
                                       krot[dc][:, :], OP.mult))
                mark(f"ksq_{i}", cV)

            @block.tensor
            def _(t, dc=dc, i=i, b=b):
                t.wait_ge(cV.sem, marks[f"ksq_{i}"])
                if dc == 0:
                    if kside and b > 0:
                        t.wait_ge(cV.sem, marks[f"sfold_{16*(b-1)+15}"])
                    if not kside:
                        if b > 0:
                            t.wait_ge(cV.sem, marks[f"z1_{4*(b-1)+3}"])
                        else:
                            t.wait_ge(cV.sem, marks["sfold_63"])
                cT.inc(t.matmul(PS[4][0:16, :], selnq[:, 16 * dc:16 * dc + 16],
                                kr2[dc % 2][:, :], start=(dc == 0), stop=(dc == 7),
                                skip_group_check=True))
                mark(f"nmm_{i}", cT)

    for b in range(NB):
        c0 = BLK * b
        emit_qk_side(b, 0, c0)

        @block.vector
        def _(v, b=b):
            v.wait_ge(cT.sem, marks[f"nmm_{8*b+7}"])
            cV.inc(v.tensor_copy(nkc[:, :], PS[4][0:16, :]))
            mark(f"nkc_{b}", cV)

        for tc in range(4):
            @block.tensor
            def _(t, b=b, tc=tc):
                t.wait_ge(cV.sem, marks[f"nkc_{b}"])
                if b > 0 and tc == 0:
                    t.wait_ge(cV.sem, marks[f"sfold_{16*(b-1)+15}"])
                cT.inc(t.transpose(PS[5][0:128, 32 * tc:32 * tc + 16],
                                   nkc[0:16, 128 * tc:128 * tc + 128],
                                   idnf[0:16, 0:16]))
                mark(f"nkt_{b}_{tc}", cT)

        @block.scalar
        def _(sc, b=b):
            sc.wait_ge(cT.sem, marks[f"nkt_{b}_3"])
            last = None
            for tc in range(4):
                last = sc.activation(wnk[tc][:, 0:16],
                                     PS[5][0:128, 32 * tc:32 * tc + 16],
                                     AF.Exp, bias=0.0, scale=-0.5)
            cS.inc(last)
            mark(f"wnk_{b}", cS)

        for tc in range(4):
            @block.tensor
            def _(t, b=b, tc=tc, c0=c0):
                i = 4 * b + tc
                reg = 2 * (tc % 2)
                if b == 0 and tc == 0:
                    t.wait_ge(cW.sem, marks["wv"])
                t.wait_ge(cS.sem, marks[f"wnk_{b}"])
                if i >= 2:
                    t.wait_ge(cV.sem, marks[f"vaug_{i-2}"])
                tok = c0 + 128 * tc
                m = None
                for kc in range(8):
                    t.matmul(PS[reg][:, :], xT[kc][:, tok:tok + 128],
                             WB[kc][:, 0:512], start=(kc == 0), stop=(kc == 7))
                for kc in range(8):
                    m = t.matmul(PS[reg + 1][:, :], xT[kc][:, tok:tok + 128],
                                 WB[kc][:, 512:1024], start=(kc == 0), stop=(kc == 7))
                cT.inc(m)
                mark(f"vmm_{i}", cT)

            @block.scalar
            def _(sc, b=b, tc=tc):
                i = 4 * b + tc
                reg = 2 * (tc % 2)
                sc.wait_ge(cT.sem, marks[f"vmm_{i}"])
                d0 = vaug[tc][:, 0:1040].rearrange("p (h c) -> p h c", h=16)
                sc.copy(d0[:, 0:8, 0:64],
                        PS[reg][:, :].rearrange("p (h c) -> p h c", h=8))
                cS.inc(sc.copy(d0[:, 8:16, 0:64],
                               PS[reg + 1][:, :].rearrange("p (h c) -> p h c", h=8)))
                mark(f"vev_{i}", cS)

            @block.vector
            def _(v, b=b, tc=tc):
                i = 4 * b + tc
                v.wait_ge(cS.sem, marks[f"vev_{i}"])
                v.wait_ge(cS.sem, marks[f"wnk_{b}"])
                va = vaug[tc][:, 0:1040].rearrange("p (h c) -> p h c", h=16)
                vchain(v, v.memset(va[:, :, 64:65], 1.0))
                wb = wnk[tc][:, 0:16].unsqueeze(2).broadcast_to([128, 16, 65])
                cV.inc(v.tensor_tensor(va, va, wb, OP.mult))
                mark(f"vaug_{i}", cV)

        for h in range(H):
            i = 16 * b + h
            reg = h % 2

            @block.tensor
            def _(t, b=b, h=h, i=i, reg=reg):
                if h < 2:
                    t.wait_ge(cV.sem, marks[f"vaug_{4*b+3}"])
                    if b > 0:
                        t.wait_ge(cV.sem, marks[f"sfold_{16*(b-1)+14+h}"])
                else:
                    t.wait_ge(cV.sem, marks[f"sfold_{i-2}"])
                m = None
                for tc in range(4):
                    m = t.matmul(PS[4 + reg][0:65, 0:256],
                                 vaug[tc][:, 65 * h:65 * h + 65],
                                 kphi[tc][:, 256 * h:256 * h + 256],
                                 start=(tc == 0), stop=(tc == 3))
                cT.inc(m)
                mark(f"smm_{i}", cT)

            @block.vector
            def _(v, b=b, h=h, i=i, reg=reg):
                v.wait_ge(cT.sem, marks[f"smm_{i}"])
                if b == 0:
                    cV.inc(v.tensor_copy(Ssb[h][0:65, :], PS[4 + reg][0:65, 0:256]))
                else:
                    cV.inc(v.tensor_tensor(Ssb[h][0:65, :], PS[4 + reg][0:65, 0:256],
                                           Ssb[h][0:65, :], OP.add))
                mark(f"sfold_{i}", cV)
    mark("S_done", cV)

    @block.gpsimd
    def _(g):
        g.wait_ge(cT.sem, marks["kmm_31"])
        g.wait_ge(cW.sem, cW.n)
        for k in range(8):
            g.dma_start(out=WA[k][:], in_=WQ[128 * k:128 * (k + 1), :]).then_inc(cW.sem, 16)
            cW.n += 16
        mark("wq", cW)
        g.wait_ge(cT.sem, marks["vmm_15"])
        g.wait_ge(cW.sem, cW.n)
        for k in range(8):
            g.dma_start(out=WB[k][:], in_=WO[128 * k:128 * (k + 1), :]).then_inc(cW.sem, 16)
            cW.n += 16
        mark("wo", cW)
        g.wait_ge(cV.sem, marks["S_done"])
        g.wait_ge(cW.sem, cW.n)
        for h in range(H):
            g.dma_start(out=SBIN.ap()[128 * h:128 * (h + 1), :],
                        in_=Ssb[h][:, :]).then_inc(cW.sem, 16)
            cW.n += 16
        mark("sbin", cW)
        g.wait_ge(cW.sem, marks["sbin"])
        cC.inc(g.collective_compute(
            "AllReduce", OP.add,
            replica_groups=[[0, 1, 2, 3], [4, 5, 6, 7]],
            ins=[SBIN.ap().opt()], outs=[SBOUT.ap().opt()]))
        mark("s_ar", cC)

    def emit_sred():
        @block.gpsimd
        def _(g):
            g.wait_ge(cC.sem, marks["s_ar"])
            g.wait_ge(cW.sem, cW.n)
            for h in range(H):
                g.dma_start(out=Ssb[h][:, :],
                            in_=SBOUT.ap()[128 * h:128 * (h + 1), :]
                            ).then_inc(cW.sem, 16)
                cW.n += 16
            mark("sred", cW)

    PTF2A = PTR[:, :].bitcast(F32)
    lnq = []

    def emit_ytt(i):
        @block.tensor
        def _(t, i=i):
            t.wait_ge(cV.sem, marks[f"lnapply_{i}"])
            if i >= 1:
                t.wait_ge(cV.sem, marks[f"ytev2_{i-1}"])
            else:
                t.wait_ge(cV.sem, marks["xT_done"])
            last = None
            for kc in range(8):
                last = t.transpose(
                    PTF2A[:, 128 * kc:128 * kc + 128],
                    ytr[i % 3][:, 128 * kc:128 * kc + 128], idnf[:, :])
            cT.inc(last)
            mark(f"ytt_{i}", cT)

        @block.vector
        def _(v, i=i):
            v.wait_ge(cT.sem, marks[f"ytt_{i}"])
            cV.inc(v.tensor_copy(
                XTB.rearrange("p (a t) -> p a t", a=8)[:, 0:8,
                                                       128 * i:128 * i + 128],
                PTF2A[:, 0:1024].rearrange("p (a t) -> p a t", a=8)))
            mark(f"ytev2_{i}", cV)

    for b in range(NB):
        c0 = BLK * b
        emit_qk_side(b, 32, c0)
        if b == 0:
            emit_sred()
        if b == 3:
            @block.gpsimd
            def _(g):
                g.wait_ge(cT.sem, marks["kmm_63"])
                g.wait_ge(cW.sem, cW.n)
                for k in range(8):
                    g.dma_start(out=WA[k][:],
                                in_=WR[128 * k:128 * (k + 1), :]).then_inc(cW.sem, 16)
                    cW.n += 16
                mark("wr", cW)

        if b == 0:
            for h in range(H):
                @block.tensor
                def _(t, h=h):
                    if h == 0:
                        t.wait_ge(cW.sem, marks["sred"])
                        t.wait_ge(cV.sem, marks["rope_39"])
                    if h >= 2:
                        t.wait_ge(cV.sem, marks[f"sev_{h-2}"])
                    t.transpose(PS[h % 2][:, 0:128], Ssb[h][:, 0:128], idnf[:, :])
                    cT.inc(t.transpose(PS[h % 2][:, 128:256], Ssb[h][:, 128:256],
                                       idnf[:, :]))
                    mark(f"str_{h}", cT)

                @block.vector
                def _(v, h=h):
                    v.wait_ge(cT.sem, marks[f"str_{h}"])
                    for m in range(2):
                        v.tensor_copy(saug[m][:, 65 * h:65 * h + 65],
                                      PS[h % 2][:, 128 * m:128 * m + 65])
                    v.tensor_copy(skc[0][:, h:h + 1], PS[h % 2][:, 64:65])
                    cV.inc(v.tensor_copy(skc[1][:, h:h + 1], PS[h % 2][:, 192:193]))
                    mark(f"sev_{h}", cV)
            mark("saug_done", cV)

        @block.scalar
        def _(sc, b=b):
            sc.wait_ge(cT.sem, marks[f"nmm_{32+8*b+7}"])
            if b > 0:
                sc.wait_ge(cV.sem, marks[f"g_{b-1}"])
            cS.inc(sc.activation(enq[:, :], PS[4][0:16, :], AF.Exp,
                                 bias=0.0, scale=-0.5))
            mark(f"enq_{b}", cS)

        @block.tensor
        def _(t, b=b):
            t.wait_ge(cS.sem, marks[f"pe_{32+8*b+7}"])
            t.wait_ge(cS.sem, marks[f"enq_{b}"])
            if b > 0:
                t.wait_ge(cV.sem, marks[f"z1_{4*(b-1)+3}"])
            else:
                t.wait_ge(cV.sem, marks["saug_done"])
            first = True
            m = None
            for h in range(H):
                for mc in range(2):
                    m = t.matmul(PS[5][0:16, :], skc[mc][:, :],
                                 qexp[2 * h + mc][:, :],
                                 start=first, stop=(h == H - 1 and mc == 1))
                    first = False
            cT.inc(m)
            mark(f"draw_{b}", cT)

        @block.vector
        def _(v, b=b):
            v.wait_ge(cT.sem, marks[f"draw_{b}"])
            v.wait_ge(cS.sem, marks[f"enq_{b}"])
            vchain(v, v.tensor_tensor(gch[0][:, :], PS[5][0:16, :], enq[:, :],
                                      OP.mult))
            vchain(v, v.tensor_scalar(gch[1][:, :], gch[0][:, :], MEPS, None,
                                      OP.max))
            vchain(v, v.reciprocal(gch[0][:, :], gch[1][:, :]))
            cV.inc(v.tensor_tensor(gsb[:, :], gch[0][:, :], enq[:, :], OP.mult))
            mark(f"g_{b}", cV)

        for p in range(8):
            i = 8 * b + p
            reg = p % 2

            @block.tensor
            def _(t, b=b, p=p, i=i, reg=reg):
                t.wait_ge(cV.sem, marks[f"g_{b}"])
                if p >= 2:
                    t.wait_ge(cV.sem, marks[f"oh_{i-2}"])
                elif b > 0:
                    t.wait_ge(cV.sem, marks[f"oh_{8*(b-1)+6+p}"])
                t.matmul(PS[2 + reg][:, :], selg[:, 128 * p:128 * p + 128],
                         gsb[:, :], start=True, stop=True)
                m = None
                for par in range(2):
                    h = 2 * p + par
                    lo = 64 * par
                    for mc in range(2):
                        m = t.matmul(PS[reg][lo:lo + 64, :],
                                     saug[mc][:, 65 * h:65 * h + 64],
                                     qexp[2 * h + mc][:, :],
                                     start=(mc == 0), stop=(mc == 1))
                cT.inc(m)
                mark(f"numer_{i}", cT)

            @block.scalar
            def _(sc, i=i, reg=reg):
                sc.wait_ge(cT.sem, marks[f"numer_{i}"])
                cS.inc(sc.copy(gx[reg][:, :], PS[2 + reg][:, :]))
                mark(f"gxe_{i}", cS)

            @block.vector
            def _(v, p=p, i=i, reg=reg):
                v.wait_ge(cS.sem, marks[f"gxe_{i}"])
                cV.inc(v.tensor_tensor(oh[p][:, :], PS[reg][:, :], gx[reg][:, :],
                                       OP.mult))
                mark(f"oh_{i}", cV)

        for tc in range(4):
            i = 4 * b + tc
            if i >= 3:
                emit_ytt(i - 3)

            @block.sync
            def _(s, b=b, tc=tc, i=i):
                if i < 2:
                    s.wait_ge(cT.sem, marks["smm_63"])
                else:
                    s.wait_ge(cV.sem, marks[f"z1_{i-2}"])
                s.wait_ge(cXF[i % 2].sem, cXF[i % 2].n)
                s.dma_start(out=xf2[tc % 2][:, :],
                            in_=X[BLK * b + 128 * tc:BLK * b + 128 * tc + 128, :]
                            ).then_inc(cXF[i % 2].sem, 16)
                cXF[i % 2].n += 16
                mark(f"xf_{i}", cXF[i % 2])

            @block.tensor
            def _(t, b=b, tc=tc, i=i):
                t.wait_ge(cV.sem, marks[f"oh_{8*b+7}"])
                reg = [4, 0, 2, 4][tc]
                if b == 0 and tc == 0:
                    t.wait_ge(cW.sem, marks["wo"])
                if tc == 0 and b > 0:
                    t.wait_ge(cV.sem, marks[f"z1_{4*(b-1)+3}"])
                elif tc == 3:
                    t.wait_ge(cV.sem, marks[f"z1_{4*b}"])
                m = None
                for p in range(8):
                    t.matmul(PS[reg][:, :], oh[p][:, 128 * tc:128 * tc + 128],
                             WB[p][:, 0:512], start=(p == 0), stop=(p == 7))
                for p in range(8):
                    m = t.matmul(PS[reg + 1][:, :], oh[p][:, 128 * tc:128 * tc + 128],
                                 WB[p][:, 512:1024], start=(p == 0), stop=(p == 7))
                cT.inc(m)
                mark(f"oproj_{i}", cT)

            @block.vector
            def _(v, tc=tc, i=i):
                v.wait_ge(cT.sem, marks[f"oproj_{i}"])
                v.wait_ge(cXF[i % 2].sem, marks[f"xf_{i}"])
                reg = [4, 0, 2, 4][tc]
                if i >= 2:
                    v.wait_ge(cS.sem, marks[f"z1sq_{i-2}"])
                v.tensor_tensor(z1[i % 2][:, 0:512], PS[reg][:, :],
                                xf2[tc % 2][:, 0:512], OP.add)
                vchain(v, v.tensor_tensor(z1[i % 2][:, 512:1024], PS[reg + 1][:, :],
                                          xf2[tc % 2][:, 512:1024], OP.add))
                cV.inc(v.reduce_sum(stat[:, 8 * (i % 2):8 * (i % 2) + 1],
                                    z1[i % 2][:, :], axis=mybir.AxisListType.X))
                mark(f"z1_{i}", cV)

            def ln_tail(b=b, tc=tc, i=i):
                so = 8 * (i % 2)

                @block.scalar
                def _(sc, i=i, so=so):
                    sc.wait_ge(cV.sem, marks[f"z1_{i}"])
                    cS.inc(sc.activation(sqs[:, :], z1[i % 2][:, :], AF.Square,
                                         accum_out=stat[:, so + 1:so + 2]))
                    mark(f"z1sq_{i}", cS)

                @block.vector
                def _(v, i=i, so=so):
                    v.wait_ge(cS.sem, marks[f"z1sq_{i}"])
                    ci = 1.0 / E
                    vchain(v, v.tensor_scalar(stat[:, so + 2:so + 3],
                                              stat[:, so:so + 1], ci, None,
                                              OP.mult))
                    vchain(v, v.tensor_tensor(stat[:, so + 3:so + 4],
                                              stat[:, so + 2:so + 3],
                                              stat[:, so + 2:so + 3], OP.mult))
                    vchain(v, v.tensor_scalar(stat[:, so + 3:so + 4],
                                              stat[:, so + 3:so + 4], -1.0,
                                              LNEPS, OP.mult, OP.add))
                    cV.inc(v.scalar_tensor_tensor(stat[:, so + 4:so + 5],
                                                  stat[:, so + 1:so + 2], ci,
                                                  stat[:, so + 3:so + 4],
                                                  OP.mult, OP.add))
                    mark(f"var_{i}", cV)

                @block.scalar
                def _(sc, i=i, so=so):
                    sc.wait_ge(cV.sem, marks[f"var_{i}"])
                    cS.inc(sc.activation(stat[:, so + 5:so + 6],
                                         stat[:, so + 4:so + 5], AF.Sqrt))
                    mark(f"sd_{i}", cS)

                @block.vector
                def _(v, i=i, so=so):
                    v.wait_ge(cS.sem, marks[f"sd_{i}"])
                    if i >= 3:
                        v.wait_ge(cYD[(i - 3) % 2].sem,
                                  marks[f"ydma_{i-3}"])
                        v.wait_ge(cT.sem, marks[f"ytt_{i-3}"])
                    vchain(v, v.reciprocal(stat[:, so + 6:so + 7],
                                           stat[:, so + 5:so + 6]))
                    vchain(v, v.scalar_tensor_tensor(stat[:, so + 7:so + 8],
                                                     stat[:, so + 2:so + 3],
                                                     -1.0, stat[:, so + 6:so + 7],
                                                     OP.mult, OP.mult))
                    cV.inc(v.tensor_scalar(ytr[i % 3][:, :], z1[i % 2][:, :],
                                           stat[:, so + 6:so + 7],
                                           stat[:, so + 7:so + 8],
                                           OP.mult, OP.add))
                    mark(f"lnapply_{i}", cV)

                @block.sync
                def _(s, b=b, tc=tc, i=i):
                    s.wait_ge(cV.sem, marks[f"lnapply_{i}"])
                    s.wait_ge(cYD[i % 2].sem, cYD[i % 2].n)
                    s.dma_start(out=YBUF.ap()[BLK * b + 128 * tc:
                                              BLK * b + 128 * tc + 128, :],
                                in_=ytr[i % 3][:, :]).then_inc(cYD[i % 2].sem, 16)
                    cYD[i % 2].n += 16
                    mark(f"ydma_{i}", cYD[i % 2])

            if tc < 2:
                ln_tail()
            else:
                lnq.append(ln_tail)

    while lnq:
        lnq.pop(0)()
    marks["y_done0"] = cYD[0].n
    marks["y_done1"] = cYD[1].n
    emit_ytt(13)
    emit_ytt(14)
    emit_ytt(15)

    if debug:
        @block.sync
        def _(s):
            s.wait_ge(cV.sem, cV.n)
            s.wait_ge(cS.sem, cS.n)
            s.wait_ge(cYD[0].sem, marks["y_done0"])
            s.wait_ge(cYD[1].sem, marks["y_done1"])
            s.wait_ge(cX.sem, cX.n)
            for nm, srcap in (("xT0", xT[0][:, :]), ("krot0", krot[0][:, :]),
                              ("kphi00", kphi[0][:, :]), ("w00", wnk[0][:, :]),
                              ("saug0", saug[0][:, :]), ("enq0", enq[:, :]),
                              ("g0", gsb[:, :]), ("oh0", oh[0][:, :]),
                              ("y0", YBUF.ap()[0:128, :])):
                s.dma_start(out=taps[nm], in_=srcap).then_inc(cX.sem, 16)
                cX.n += 16
            mark("tapmid", cX)

    @block.gpsimd
    def _(g):
        g.wait_ge(cT.sem, marks["oproj_15"])
        g.wait_ge(cW.sem, cW.n)
        for k in range(8):
            g.dma_start(out=WB[k][:], in_=WKF[128 * k:128 * (k + 1), :]).then_inc(cW.sem, 16)
            cW.n += 16
        mark("wkf", cW)
        g.wait_ge(cW.sem, cW.n)
        for k in range(8):
            g.dma_start(out=WC[k][:], in_=WVF[128 * k:128 * (k + 1), :]).then_inc(cW.sem, 16)
            cW.n += 16
        mark("wvf", cW)

    @block.sync
    def _(s):
        s.wait_ge(cYD[0].sem, marks["y_done0"])
        s.wait_ge(cYD[1].sem, marks["y_done1"])
        s.wait_ge(cX.sem, cX.n)
        s.dma_start(out=GIN.ap(), in_=ytr[15 % 3][127:128, :]).then_inc(cX.sem, 16)
        cX.n += 16
        mark("gin", cX)

    @block.gpsimd
    def _(g):
        g.wait_ge(cX.sem, marks["gin"])
        g.wait_ge(cYD[0].sem, marks["y_done0"])
        g.wait_ge(cYD[1].sem, marks["y_done1"])
        cC.inc(g.collective_compute(
            "AllGather", OP.bypass,
            replica_groups=[[0, 1, 2, 3, 4, 5, 6, 7]],
            ins=[GIN.ap().opt()], outs=[GOUT.ap().opt()]))
        mark("ag", cC)
        g.wait_ge(cC.sem, marks["ag"])
        g.wait_ge(cW.sem, cW.n)
        g.dma_start(out=agsb[:, :], in_=GOUT.ap()).then_inc(cW.sem, 16)
        cW.n += 16
        mark("agld", cW)

    PB = [1, 2, 3, 0]
    PRP = [0, 2, 4, 0]
    PWP = [4, 0, 2, 4]

    PTF = PTR[:, :].bitcast(F32)
    for bi, b in enumerate(PB):
        c0 = BLK * b
        prev = PB[bi - 1] if bi > 0 else None
        if bi == 1:
            @block.tensor
            def _(t):
                t.wait_ge(cW.sem, marks["agld"])
                t.wait_ge(cV.sem, marks["ytev2_15"])
                t.matmul(PTF[0:1, 0:512], hsel[:, :], agsb[:, 0:512],
                         start=True, stop=True)
                cT.inc(t.matmul(PTF[0:1, 512:1024], hsel[:, :], agsb[:, 512:1024],
                                start=True, stop=True))
                mark("halomm", cT)

            @block.scalar
            def _(sc):
                sc.wait_ge(cT.sem, marks["halomm"])
                sc.copy(hrow[0:1, 0:512], PTF[0:1, 0:512])
                cS.inc(sc.copy(hrow[0:1, 512:1024], PTF[0:1, 512:1024]))
                mark("hrow", cS)

            @block.tensor
            def _(t):
                t.wait_ge(cS.sem, marks["hrow"])
                last = None
                for k in range(8):
                    last = t.matmul(PTF[:, k:k + 1],
                                    hrow[0:1, 128 * k:128 * k + 128],
                                    ones11[0:1, 0:1], start=True, stop=True)
                cT.inc(last)
                mark("halot", cT)

            @block.vector
            def _(v):
                v.wait_ge(cT.sem, marks["halot"])
                cV.inc(v.tensor_copy(haloT[:, 0:8], PTF[:, 0:8]))
                mark("haloT", cV)
        for ec in range(8):
            @block.vector
            def _(v, b=b, ec=ec, c0=c0, prev=prev):
                v.wait_ge(cV.sem, marks[f"ytev2_{4*b+3}"])
                v.wait_ge(cV.sem, cV.n)
                if b == 0 and ec == 0:
                    v.wait_ge(cV.sem, marks["haloT"])
                if prev is not None and ec == 0:
                    v.wait_ge(cT.sem, marks[f"rmm_{4*prev+3}"])
                if c0 == 0:
                    v.tensor_tensor(dsb[:, 1:BLK], yT[ec][:, 1:BLK],
                                    yT[ec][:, 0:BLK - 1], OP.subtract)
                    vchain(v, v.tensor_tensor(dsb[:, 0:1], yT[ec][:, 0:1],
                                              haloT[:, ec:ec + 1], OP.subtract))
                    v.scalar_tensor_tensor(xk[ec][:, 1:BLK], dsb[:, 1:BLK],
                                           muk[:, ec:ec + 1], yT[ec][:, 0:BLK - 1],
                                           OP.mult, OP.add)
                    cV.inc(v.scalar_tensor_tensor(xk[ec][:, 0:1], dsb[:, 0:1],
                                                  muk[:, ec:ec + 1],
                                                  haloT[:, ec:ec + 1],
                                                  OP.mult, OP.add))
                    if ec == 7:
                        mark(f"mixk_{b}", cV)
                    v.scalar_tensor_tensor(xr[ec][:, 1:BLK], dsb[:, 1:BLK],
                                           mur[:, ec:ec + 1], yT[ec][:, 0:BLK - 1],
                                           OP.mult, OP.add)
                    cV.inc(v.scalar_tensor_tensor(xr[ec][:, 0:1], dsb[:, 0:1],
                                                  mur[:, ec:ec + 1],
                                                  haloT[:, ec:ec + 1],
                                                  OP.mult, OP.add))
                else:
                    vchain(v, v.tensor_tensor(dsb[:, :], yT[ec][:, c0:c0 + BLK],
                                              yT[ec][:, c0 - 1:c0 + BLK - 1],
                                              OP.subtract))
                    cV.inc(v.scalar_tensor_tensor(xk[ec][:, :], dsb[:, :],
                                                  muk[:, ec:ec + 1],
                                                  yT[ec][:, c0 - 1:c0 + BLK - 1],
                                                  OP.mult, OP.add))
                    if ec == 7:
                        mark(f"mixk_{b}", cV)
                    cV.inc(v.scalar_tensor_tensor(xr[ec][:, :], dsb[:, :],
                                                  mur[:, ec:ec + 1],
                                                  yT[ec][:, c0 - 1:c0 + BLK - 1],
                                                  OP.mult, OP.add))
                mark(f"mix_{8*b+ec}", cV)
        mark(f"mixb_{b}", cV)

        for dc in range(8):
            i = 8 * b + dc

            @block.tensor
            def _(t, b=b, dc=dc, i=i, bi=bi, prev=prev):
                t.wait_ge(cV.sem, marks[f"mixk_{b}"])
                if bi == 0 and dc == 0:
                    t.wait_ge(cW.sem, marks["wkf"])
                if dc < 2 and prev is not None:
                    t.wait_ge(cV.sem, marks[f"z2_{4*prev+2}"])
                if dc >= 2:
                    t.wait_ge(cV.sem, marks[f"kasq_{i-2}"])
                m = None
                for ec in range(8):
                    m = t.matmul(PS[2 + dc % 2][:, :],
                                 WB[ec][:, 128 * dc:128 * dc + 128],
                                 xk[ec][:, :], start=(ec == 0), stop=(ec == 7))
                cT.inc(m)
                mark(f"kamm_{i}", cT)

            @block.vector
            def _(v, dc=dc, i=i):
                v.wait_ge(cT.sem, marks[f"kamm_{i}"])
                vchain(v, v.tensor_scalar(kr2[dc % 2][:, :], PS[2 + dc % 2][:, :],
                                          0.0, None, OP.max))
                cV.inc(v.tensor_tensor(kact[dc][:, :], kr2[dc % 2][:, :],
                                       kr2[dc % 2][:, :], OP.mult))
                mark(f"kasq_{i}", cV)

        for tc in range(4):
            i = 4 * b + tc
            pr = PRP[tc]
            pw = PWP[tc]

            @block.tensor
            def _(t, b=b, tc=tc, i=i, pr=pr, bi=bi, prev=prev):
                t.wait_ge(cV.sem, marks[f"mixb_{b}"])
                if bi == 0 and tc == 0:
                    t.wait_ge(cW.sem, marks["wr"])
                if tc == 0:
                    if prev is not None:
                        t.wait_ge(cS.sem, marks[f"rsig_{4*prev+3}"])
                elif tc == 1:
                    t.wait_ge(cV.sem, marks[f"kasq_{8*b+7}"])
                elif tc == 2:
                    t.wait_ge(cV.sem, marks[f"z2_{4*b+0}"])
                else:
                    t.wait_ge(cV.sem, marks[f"z2_{4*b+1}"])
                m = None
                for ec in range(8):
                    t.matmul(PS[pr][:, :], xr[ec][:, 128 * tc:128 * tc + 128],
                             WA[ec][:, 0:512], start=(ec == 0), stop=(ec == 7))
                for ec in range(8):
                    m = t.matmul(PS[pr + 1][:, :], xr[ec][:, 128 * tc:128 * tc + 128],
                                 WA[ec][:, 512:1024], start=(ec == 0), stop=(ec == 7))
                cT.inc(m)
                mark(f"rmm_{i}", cT)

            @block.scalar
            def _(sc, b=b, tc=tc, i=i, pr=pr, bi=bi, prev=prev):
                sc.wait_ge(cT.sem, marks[f"rmm_{i}"])
                if tc >= 2:
                    sc.wait_ge(cV.sem, marks[f"z2_{4*b+tc-2}"])
                elif prev is not None:
                    sc.wait_ge(cV.sem, marks[f"z2_{4*prev+2+tc}"])
                sc.activation(rsb[tc % 2][:, 0:512], PS[pr][:, :], AF.Sigmoid)
                cS.inc(sc.activation(rsb[tc % 2][:, 512:1024], PS[pr + 1][:, :],
                                     AF.Sigmoid))
                mark(f"rsig_{i}", cS)

            @block.tensor
            def _(t, b=b, tc=tc, i=i, pw=pw, bi=bi, prev=prev):
                t.wait_ge(cV.sem, marks[f"kasq_{8*b+7}"])
                if bi == 0 and tc == 0:
                    t.wait_ge(cW.sem, marks["wvf"])
                if tc == 0:
                    if prev is not None:
                        t.wait_ge(cV.sem, marks[f"z2_{4*prev+3}"])
                elif tc == 1:
                    t.wait_ge(cS.sem, marks[f"rsig_{4*b+0}"])
                elif tc == 2:
                    t.wait_ge(cS.sem, marks[f"rsig_{4*b+1}"])
                else:
                    t.wait_ge(cS.sem, marks[f"rsig_{4*b+2}"])
                m = None
                for dc in range(8):
                    t.matmul(PS[pw][:, :], kact[dc][:, 128 * tc:128 * tc + 128],
                             WC[dc][:, 0:512], start=(dc == 0), stop=(dc == 7))
                for dc in range(8):
                    m = t.matmul(PS[pw + 1][:, :], kact[dc][:, 128 * tc:128 * tc + 128],
                                 WC[dc][:, 512:1024], start=(dc == 0), stop=(dc == 7))
                cT.inc(m)
                mark(f"wvmm_{i}", cT)

            @block.sync
            def _(s, b=b, tc=tc, i=i, bi=bi, prev=prev):
                if tc >= 2:
                    s.wait_ge(cV.sem, marks[f"z2_{4*b+tc-2}"])
                elif prev is not None:
                    s.wait_ge(cV.sem, marks[f"z2_{4*prev+2+tc}"])
                s.wait_ge(cY2[i % 2].sem, cY2[i % 2].n)
                s.dma_start(out=y2[i % 2][:, :],
                            in_=YBUF.ap()[BLK * b + 128 * tc:BLK * b + 128 * tc + 128, :]
                            ).then_inc(cY2[i % 2].sem, 16)
                cY2[i % 2].n += 16
                mark(f"y2_{i}", cY2[i % 2])

            @block.vector
            def _(v, tc=tc, i=i, pw=pw):
                v.wait_ge(cT.sem, marks[f"wvmm_{i}"])
                v.wait_ge(cS.sem, marks[f"rsig_{i}"])
                v.wait_ge(cY2[i % 2].sem, marks[f"y2_{i}"])
                v.wait_ge(cX.sem, marks["gin"])
                if i == 4:
                    v.wait_ge(cT.sem, marks["ytt_15"])
                elif i == 5:
                    v.wait_ge(cT.sem, marks["ytt_13"])
                v.tensor_tensor(tmul[i % 2][:, 0:512], PS[pw][:, :],
                                rsb[tc % 2][:, 0:512], OP.mult)
                vchain(v, v.tensor_tensor(tmul[i % 2][:, 512:1024], PS[pw + 1][:, :],
                                          rsb[tc % 2][:, 512:1024], OP.mult))
                vchain(v, v.tensor_tensor(z2[i % 2][:, :], tmul[i % 2][:, :],
                                          y2[i % 2][:, :], OP.add))
                cV.inc(v.reduce_sum(stat[:, 0:1], z2[i % 2][:, :],
                                    axis=mybir.AxisListType.X))
                mark(f"z2_{i}", cV)

            @block.scalar
            def _(sc, i=i):
                sc.wait_ge(cV.sem, marks[f"z2_{i}"])
                cS.inc(sc.activation(sqs[:, :], z2[i % 2][:, :], AF.Square,
                                     accum_out=stat[:, 1:2]))
                mark(f"z2sq_{i}", cS)

            @block.vector
            def _(v, i=i):
                v.wait_ge(cS.sem, marks[f"z2sq_{i}"])
                ci = 1.0 / E
                vchain(v, v.tensor_scalar(stat[:, 2:3], stat[:, 0:1], ci, None,
                                          OP.mult))
                vchain(v, v.tensor_tensor(stat[:, 3:4], stat[:, 2:3],
                                          stat[:, 2:3], OP.mult))
                vchain(v, v.tensor_scalar(stat[:, 3:4], stat[:, 3:4], -1.0,
                                          LNEPS, OP.mult, OP.add))
                cV.inc(v.scalar_tensor_tensor(stat[:, 4:5], stat[:, 1:2], ci,
                                              stat[:, 3:4], OP.mult, OP.add))
                mark(f"var2_{i}", cV)

            @block.scalar
            def _(sc, i=i):
                sc.wait_ge(cV.sem, marks[f"var2_{i}"])
                cS.inc(sc.activation(stat[:, 5:6], stat[:, 4:5], AF.Sqrt))
                mark(f"sd2_{i}", cS)

            @block.vector
            def _(v, i=i, b=b, tc=tc, prev=prev):
                v.wait_ge(cS.sem, marks[f"sd2_{i}"])
                if tc >= 2:
                    v.wait_ge(cOD[i % 2].sem, marks[f"odma_{4*b+tc-2}"])
                elif prev is not None:
                    v.wait_ge(cOD[(4*prev+2+tc) % 2].sem, marks[f"odma_{4*prev+2+tc}"])
                vchain(v, v.reciprocal(stat[:, 6:7], stat[:, 5:6]))
                vchain(v, v.scalar_tensor_tensor(stat[:, 7:8], stat[:, 2:3],
                                                 -1.0, stat[:, 6:7],
                                                 OP.mult, OP.mult))
                cV.inc(v.tensor_scalar(osb[i % 2][:, :], z2[i % 2][:, :],
                                       stat[:, 6:7], stat[:, 7:8], OP.mult, OP.add))
                mark(f"lnout_{i}", cV)

            @block.sync
            def _(s, b=b, tc=tc, i=i):
                s.wait_ge(cV.sem, marks[f"lnout_{i}"])
                s.wait_ge(cOD[i % 2].sem, cOD[i % 2].n)
                s.dma_start(out=OUT[BLK * b + 128 * tc:BLK * b + 128 * tc + 128, :],
                            in_=osb[i % 2][:, :]).then_inc(cOD[i % 2].sem, 16)
                cOD[i % 2].n += 16
                mark(f"odma_{i}", cOD[i % 2])

    if debug:
        @block.sync
        def _(s):
            s.wait_ge(cOD[0].sem, cOD[0].n)
            s.wait_ge(cOD[1].sem, cOD[1].n)
            s.wait_ge(cV.sem, cV.n)
            s.wait_ge(cS.sem, cS.n)
            s.wait_ge(cX.sem, cX.n)
            for nm, srcap in (("mt", mt[:, :]), ("xr0", xr[0][:, :]),
                              ("kact0", kact[0][:, :]), ("z20", z2[0][:, :])):
                s.dma_start(out=taps[nm], in_=srcap).then_inc(cX.sem, 16)
                cX.n += 16

    @block.sync
    def _(s):
        s.wait_ge(cOD[0].sem, cOD[0].n)
        s.wait_ge(cOD[1].sem, cOD[1].n)
        s.wait_ge(cX.sem, cX.n)

    return nc


def _in_maps(inputs):
    x = np.ascontiguousarray(np.asarray(inputs["L0_attn_x"], dtype=np.float32))
    w_qkv = np.asarray(inputs["L0_attn_w_qkv"], dtype=np.float32)
    w_out = np.asarray(inputs["L0_attn_w_out"], dtype=np.float32)
    signs = np.asarray(inputs["L0_attn_sorf_signs"], dtype=np.float32)
    scales = np.asarray(inputs["L0_attn_sorf_scales"], dtype=np.float32)
    W_r = np.asarray(inputs["L0_ffn_W_r"], dtype=np.float32)
    W_k = np.asarray(inputs["L0_ffn_W_k"], dtype=np.float32)
    W_v = np.asarray(inputs["L0_ffn_W_v"], dtype=np.float32)
    mu_r = np.asarray(inputs["L0_ffn_mu_r"], dtype=np.float32)
    mu_k = np.asarray(inputs["L0_ffn_mu_k"], dtype=np.float32)

    consts = _host_consts()
    shared = {
        "wq": np.ascontiguousarray(w_qkv[0:E, :].T),
        "wk": np.ascontiguousarray(w_qkv[E:2 * E, :].T),
        "wv": np.ascontiguousarray(w_qkv[2 * E:3 * E, :].T),
        "wo": np.ascontiguousarray(w_out.T),
        "wr": np.ascontiguousarray(W_r.T),
        "wkf": np.ascontiguousarray(W_k.T),
        "wvf": np.ascontiguousarray(W_v.T),
        "mur": np.ascontiguousarray(mu_r.reshape(8, 128).T),
        "muk": np.ascontiguousarray(mu_k.reshape(8, 128).T),
        "scl": np.ascontiguousarray(np.hstack([scales.reshape(64, 64)] * 2)),
        "h64": consts["h64"], "i64rep": consts["i64rep"],
        "idnb": consts["idnb"], "idnf": consts["idnf"],
        "selnq": consts["selnq"], "selg": consts["selg"],
    }
    for s in range(3):
        shared[f"sg{s+1}"] = np.ascontiguousarray(
            np.hstack([signs[:, :, s, :].reshape(64, 64)] * 2))

    in_maps = []
    for i in range(NCORES):
        c, j = divmod(i, 4)
        t0 = j * T
        Ct, St = _rope_tables(t0)
        hs = np.zeros((8, 1), dtype=np.float32)
        if j > 0:
            hs[i - 1, 0] = 1.0
        m = dict(shared)
        m["x"] = np.ascontiguousarray(x[c, t0:t0 + T, :])
        m["ropec"] = Ct
        m["ropes"] = St
        m["hsel"] = hs
        in_maps.append(m)
    return in_maps


def kernel(**inputs):
    in_maps = _in_maps(inputs)
    nc = build_nc(debug=False)
    res = run_bass_kernel_spmd(nc, in_maps, core_ids=list(range(NCORES)))
    out = np.empty((2, 8192, E), dtype=np.float32)
    for i in range(NCORES):
        c, j = divmod(i, 4)
        out[c, j * T:(j + 1) * T, :] = res.results[i]["out"]
    return out


# version 14
# speedup vs baseline: 1.0570x; 1.0238x over previous
import math
from contextlib import ExitStack

import numpy as np
import ml_dtypes

import concourse.bass as bass
import concourse.mybir as mybir
from concourse.bass_utils import run_bass_kernel_spmd

F32 = mybir.dt.float32
BF16 = mybir.dt.bfloat16
AF = mybir.ActivationFunctionType
OP = mybir.AluOpType
BFNP = ml_dtypes.bfloat16

NCORES = 8
T = 2048
NB = 4
BLK = 512
NTC = 16
E = 1024
H = 16
D = 64
MF = 256
MEPS = 256.0 * 1e-6
LNEPS = 1e-5
SWAP32 = [i ^ 1 for i in range(32)]


def _hadamard(n):
    m = np.array([[1.0]], dtype=np.float32)
    while m.shape[0] < n:
        m = np.block([[m, m], [m, -m]])
    return m


def _host_consts():
    c = {}
    H64 = _hadamard(64)
    c["h64"] = np.vstack([H64, H64]).astype(BFNP)
    I64 = np.eye(64, dtype=np.float32)
    rep = np.hstack([I64] * 8)
    c["i64rep"] = np.vstack([rep, rep]).astype(np.float32)
    c["idnb"] = np.eye(128, dtype=np.float32).astype(BFNP)
    c["idnf"] = np.eye(128, dtype=np.float32)
    selnq = np.zeros((128, 128), dtype=np.float32)
    for dc in range(8):
        selnq[0:64, 16 * dc + 2 * dc] = 1.0
        selnq[64:128, 16 * dc + 2 * dc + 1] = 1.0
    c["selnq"] = selnq.astype(BFNP)
    selg = np.zeros((16, 1024), dtype=np.float32)
    for p in range(8):
        selg[2 * p, 128 * p:128 * p + 64] = 1.0
        selg[2 * p + 1, 128 * p + 64:128 * p + 128] = 1.0
    c["selg"] = selg.astype(BFNP)
    return c


def _rope_tables(t0):
    d = np.arange(D)
    freqs = 10000.0 ** (-(2 * (d // 2)) / D)
    g = np.arange(t0, t0 + T, dtype=np.float64)
    ang = np.where(g[None, :] > 0, (g[None, :] - 1) * freqs[:, None], 0.0)
    Ct = np.where(g[None, :] > 0, np.cos(ang), 1.0)
    sgn = np.where(d % 2 == 0, -1.0, 1.0)[:, None]
    St = np.where(g[None, :] > 0, sgn * np.sin(ang), 0.0)
    return (np.vstack([Ct, Ct]).astype(BFNP), np.vstack([St, St]).astype(BFNP))


class _Ctr:
    def __init__(self, ctx, nc, name):
        self.sem = ctx.enter_context(nc.semaphore(name))
        self.n = 0

    def inc(self, instr, k=1):
        instr.then_inc(self.sem, k)
        self.n += k
        return self.n


def build_nc(debug=False):
    nc = bass.Bass("TRN2", target_bir_lowering=False)

    def din(name, shape, dtype=F32):
        return nc.dram_tensor(name, shape, dtype, kind="ExternalInput").ap()

    X = din("x", [T, E])
    WQ = din("wq", [E, E]); WK = din("wk", [E, E]); WV = din("wv", [E, E])
    WO = din("wo", [E, E])
    WR = din("wr", [E, E]); WKF = din("wkf", [E, E]); WVF = din("wvf", [E, E])
    MUR = din("mur", [128, 8]); MUK = din("muk", [128, 8])
    SG = [din(f"sg{s}", [64, 128]) for s in (1, 2, 3)]
    SCL = din("scl", [64, 128])
    H64 = din("h64", [128, 64], BF16)
    I64R = din("i64rep", [128, 512])
    IDNB = din("idnb", [128, 128], BF16)
    IDNF = din("idnf", [128, 128])
    SELNQ = din("selnq", [128, 128], BF16)
    SELG = din("selg", [16, 1024], BF16)
    ROPEC = din("ropec", [128, T], BF16)
    ROPES = din("ropes", [128, T], BF16)
    HSEL = din("hsel", [8, 1])
    OUT = nc.dram_tensor("out", [T, E], F32, kind="ExternalOutput").ap()

    YBUF = nc.dram_tensor("ybuf", [T, E], F32, kind="Internal")
    SBIN = nc.dram_tensor("sbin", [H * 128, MF], BF16, kind="Internal")
    SBOUT = nc.dram_tensor("sbout", [H * 128, MF], BF16, kind="Internal")
    GIN = nc.dram_tensor("gin", [1, E], F32, kind="Internal")
    GOUT = nc.dram_tensor("gout", [8, E], F32, kind="Internal")

    taps = {}
    if debug:
        def tap(name, shape, dtype=F32):
            taps[name] = nc.dram_tensor("tap_" + name, shape, dtype,
                                        kind="ExternalOutput").ap()
        tap("mt", [128, 4096], BF16); tap("xT0", [128, T], BF16)
        tap("krot0", [128, BLK], BF16); tap("kphi00", [128, 4096], BF16)
        tap("w00", [128, 16]); tap("saug0", [128, 1040], BF16)
        tap("enq0", [16, BLK]); tap("g0", [16, BLK], BF16)
        tap("oh0", [128, BLK], BF16); tap("y0", [128, E])
        tap("xr0", [128, BLK], BF16); tap("kact0", [128, BLK], BF16)
        tap("z20", [128, E])

    ctx = ExitStack()

    def sb(name, shape, dtype):
        return ctx.enter_context(nc.sbuf_tensor(name, shape, dtype))

    def psa(name, shape, dtype=F32):
        return ctx.enter_context(nc.psum_tensor(name, shape, dtype))

    XTB = sb("xtb_s", [128, 8 * T], BF16)
    xT = [XTB[:, T * k:T * (k + 1)] for k in range(8)]
    yT = xT
    xb = [sb(f"xb{t}", [128, E], BF16) for t in range(8)]
    WA = [sb(f"wA{k}", [128, E], BF16) for k in range(8)]
    WB = [sb(f"wB{k}", [128, E], BF16) for k in range(8)]
    mt = sb("mt_s", [128, 4096], BF16)
    ropec = sb("ropec_s", [128, T], BF16)
    ropes = sb("ropes_s", [128, T], BF16)
    idnb = sb("idnb_s", [128, 128], BF16)
    idnf = sb("idnf_s", [128, 128], F32)
    selnq = sb("selnq_s", [128, 128], BF16)
    selg = sb("selg_s", [16, 1024], BF16)
    h64 = sb("h64_s", [128, 64], BF16)
    i64r = sb("i64r", [128, 512], F32)
    sg = [sb(f"sgs{s}", [64, 128], F32) for s in range(3)]
    scl = sb("scl_s", [64, 128], F32)
    sgT = [sb(f"sgT{s}", [128, 64], F32) for s in range(3)]
    gT = sb("gT", [128, 64], F32)
    r0 = sb("r0", [128, 512], BF16)
    rt = sb("rt", [128, 512], BF16)
    mur = sb("mur_s", [128, 8], F32)
    muk = sb("muk_s", [128, 8], F32)
    hsel = sb("hsel_s", [8, 1], F32)

    ksb = [sb(f"ksb{i}", [128, BLK], F32) for i in range(2)]
    krot = [sb(f"krot{d}", [128, BLK], BF16) for d in range(8)]
    xr = krot
    kr2 = [sb(f"kr2{i}", [128, BLK], BF16) for i in range(2)]
    nkc = sb("nkc", [16, BLK], F32)
    wnk = [sb(f"wnk{t}", [128, 16], F32) for t in range(4)]
    AR1 = sb("AR1", [128, 16384], BF16)
    kphi = [AR1[:, 4096 * t:4096 * (t + 1)] for t in range(4)]
    qexp = [AR1[:, 512 * i:512 * (i + 1)] for i in range(32)]
    xk = [AR1[:, 512 * k:512 * (k + 1)] for k in range(8)]
    kact = [AR1[:, 4096 + 512 * k:4096 + 512 * (k + 1)] for k in range(8)]
    WC = [AR1[:, 8192 + 1024 * k:8192 + 1024 * (k + 1)] for k in range(8)]
    SSA = sb("SSA", [128, 4096], F32)
    Ssb = [SSA[:, MF * h:MF * (h + 1)] for h in range(H)]
    z1 = [SSA[:, 1024 * i:1024 * (i + 1)] for i in range(2)]
    y2 = [SSA[:, 1024 * i:1024 * (i + 1)] for i in range(2)]
    sqs = SSA[:, 2048:3072]
    SSAB16 = SSA[:, :].bitcast(BF16)
    VAA = sb("VAA", [128, 4160], BF16)
    vaug = [VAA[:, 1040 * t:1040 * (t + 1)] for t in range(4)]
    VAF = VAA[:, :].bitcast(F32)
    xf2 = [VAF[:, 1024 * i:1024 * (i + 1)] for i in range(2)]
    osb = [VAF[:, 1024 * i:1024 * (i + 1)] for i in range(2)]
    saug = [sb(f"saug{m}", [128, 1040], BF16) for m in range(2)]
    skc = [sb(f"skc{m}", [128, 16], BF16) for m in range(2)]
    GAR = sb("GAR", [128, 2304], F32)
    GARB = GAR[:, :].bitcast(BF16)
    enq = GAR[0:16, 0:512]
    gch = [GAR[0:16, 512 + 512 * i:1024 + 512 * i] for i in range(2)]
    gsb = GARB[0:16, 3072:3584]
    gx = [GARB[:, 3584 + 512 * i:4096 + 512 * i] for i in range(2)]
    tmul = [GAR[:, 1024 * i:1024 * (i + 1)] for i in range(2)]
    dsb = GARB[:, 4096:4608]
    OHA = sb("OHA", [128, 4096], BF16)
    oh = [OHA[:, 512 * p:512 * (p + 1)] for p in range(8)]
    rsb = [OHA[:, 1024 * i:1024 * (i + 1)] for i in range(2)]
    OHAF = OHA[:, :].bitcast(F32)
    agsb = OHAF[0:8, 1024:2048]
    hrow = SSAB16[0:1, 6144:7168]
    stat = sb("stat", [128, 16], F32)
    ytr = [sb(f"ytr{i}", [128, E], F32) for i in range(2)]
    z2 = ytr[0:2]
    ytr.append(SSA[:, 3072:4096])
    haloT = sb("haloT", [128, 8], BF16)
    ones11 = sb("ones11", [1, 1], BF16)

    PS = [psa(f"PS{i}", [128, 512], F32) for i in range(6)]
    PTR = psa("PTR", [128, 2048], BF16)

    cW = _Ctr(ctx, nc, "cW")
    cX = _Ctr(ctx, nc, "cX")
    cT = _Ctr(ctx, nc, "cT")
    cV = _Ctr(ctx, nc, "cV")
    cS = _Ctr(ctx, nc, "cS")
    cG = _Ctr(ctx, nc, "cG")
    cC = _Ctr(ctx, nc, "cC")
    cXF = [_Ctr(ctx, nc, f"cXF{i}") for i in range(2)]
    cYD = [_Ctr(ctx, nc, f"cYD{i}") for i in range(2)]
    cY2 = [_Ctr(ctx, nc, f"cY2{i}") for i in range(2)]
    cOD = [_Ctr(ctx, nc, f"cOD{i}") for i in range(2)]

    block = ctx.enter_context(nc.Block())
    marks = {}

    def mark(name, ctr):
        marks[name] = ctr.n

    def vchain(v, instr):
        cV.inc(instr)
        v.wait_ge(cV.sem, cV.n)

    @block.gpsimd
    def _(g):
        for s, dst in ((IDNB, idnb), (IDNF, idnf), (SELNQ, selnq),
                       (SELG, selg), (H64, h64), (I64R, i64r),
                       (ROPEC, ropec), (ROPES, ropes),
                       (MUR, mur), (MUK, muk), (HSEL, hsel),
                       (SG[0], sg[0]), (SG[1], sg[1]), (SG[2], sg[2]), (SCL, scl)):
            g.dma_start(out=dst[:], in_=s[:]).then_inc(cW.sem, 16)
            cW.n += 16
        mark("consts", cW)
        g.wait_ge(cW.sem, cW.n)
        for t in range(8):
            g.dma_start(out=xb[t][:], in_=X[128 * t:128 * (t + 1), :]).then_inc(cW.sem, 16)
            cW.n += 16
        mark("xbg_0", cW)
        g.wait_ge(cW.sem, cW.n)
        for k in range(8):
            g.dma_start(out=WA[k][:], in_=WK[128 * k:128 * (k + 1), :]).then_inc(cW.sem, 16)
            cW.n += 16
        mark("wk", cW)
        g.wait_ge(cW.sem, cW.n)
        for k in range(8):
            g.dma_start(out=WB[k][:], in_=WV[128 * k:128 * (k + 1), :]).then_inc(cW.sem, 16)
            cW.n += 16
        mark("wv", cW)
        g.memset(ones11[:], 1.0)
        for m in range(2):
            g.memset(skc[m][:], 0.0)
        for h in range(H):
            g.memset(Ssb[h][64:128, :], 0.0)
        g.memset(haloT[:], 0.0).then_inc(cG.sem, 1)
        cG.n += 1
        mark("memsets", cG)

    @block.tensor
    def _(t):
        t.wait_ge(cW.sem, marks["consts"])
        t.wait_ge(cG.sem, marks["memsets"])
        for s in range(3):
            t.transpose(PS[s][0:128, 0:64], sg[s][:, :], idnf[0:64, 0:64])
        cT.inc(t.transpose(PS[3][0:128, 0:64], scl[:, :], idnf[0:64, 0:64]))
        mark("sgT_ps", cT)

    @block.vector
    def _(v):
        v.wait_ge(cT.sem, marks["sgT_ps"])
        for s in range(3):
            v.tensor_copy(sgT[s][:, :], PS[s][0:128, 0:64])
        cV.inc(v.tensor_scalar(gT[:, :], PS[3][0:128, 0:64], 1.0 / 4096.0,
                               None, OP.mult))
        mark("sgT", cV)

    for b in range(8):
        @block.vector
        def _(v, b=b):
            v.wait_ge(cV.sem, marks["sgT"])
            if b >= 1:
                v.wait_ge(cT.sem, marks[f"mb_mm2_{b-1}"])
            gbc = gT[:, 8 * b:8 * b + 8].unsqueeze(2).broadcast_to([128, 8, 64])
            cV.inc(v.tensor_tensor(r0[:, :], i64r[:, :], gbc, OP.mult))
            mark(f"mb_r0_{b}", cV)

        @block.tensor
        def _(t, b=b):
            t.wait_ge(cV.sem, marks[f"mb_r0_{b}"])
            t.matmul(PS[4][0:64, :], h64[0:64, :], r0[0:64, :], start=True, stop=True)
            cT.inc(t.matmul(PS[4][64:128, :], h64[64:128, :], r0[64:128, :],
                            start=True, stop=True, tile_position=(64, 64)))
            mark(f"mb_mm0_{b}", cT)

        @block.vector
        def _(v, b=b):
            v.wait_ge(cT.sem, marks[f"mb_mm0_{b}"])
            sbc = sgT[2][:, 8 * b:8 * b + 8].unsqueeze(2).broadcast_to([128, 8, 64])
            cV.inc(v.tensor_tensor(rt[:, :], PS[4][:, :], sbc, OP.mult))
            mark(f"mb_r3_{b}", cV)

        @block.tensor
        def _(t, b=b):
            t.wait_ge(cV.sem, marks[f"mb_r3_{b}"])
            t.matmul(PS[5][0:64, :], h64[0:64, :], rt[0:64, :], start=True, stop=True)
            cT.inc(t.matmul(PS[5][64:128, :], h64[64:128, :], rt[64:128, :],
                            start=True, stop=True, tile_position=(64, 64)))
            mark(f"mb_mm1_{b}", cT)

        @block.vector
        def _(v, b=b):
            v.wait_ge(cT.sem, marks[f"mb_mm1_{b}"])
            sbc = sgT[1][:, 8 * b:8 * b + 8].unsqueeze(2).broadcast_to([128, 8, 64])
            cV.inc(v.tensor_tensor(rt[:, :], PS[5][:, :], sbc, OP.mult))
            mark(f"mb_r2_{b}", cV)

        @block.tensor
        def _(t, b=b):
            t.wait_ge(cV.sem, marks[f"mb_r2_{b}"])
            t.matmul(PS[4][0:64, :], h64[0:64, :], rt[0:64, :], start=True, stop=True)
            cT.inc(t.matmul(PS[4][64:128, :], h64[64:128, :], rt[64:128, :],
                            start=True, stop=True, tile_position=(64, 64)))
            mark(f"mb_mm2_{b}", cT)

        @block.vector
        def _(v, b=b):
            v.wait_ge(cT.sem, marks[f"mb_mm2_{b}"])
            sbc = sgT[0][:, 8 * b:8 * b + 8].unsqueeze(2).broadcast_to([128, 8, 64])
            cV.inc(v.tensor_tensor(mt[:, 512 * b:512 * b + 512], PS[4][:, :],
                                   sbc, OP.mult))
            mark(f"mb_done_{b}", cV)
    mark("mt_done", cV)

    for grp in range(2):
        if grp == 1:
            @block.gpsimd
            def _(g):
                g.wait_ge(cT.sem, marks["xtr_7"])
                g.wait_ge(cW.sem, cW.n)
                for t8 in range(8):
                    g.dma_start(out=xb[t8][:],
                                in_=X[128 * (8 + t8):128 * (9 + t8), :]
                                ).then_inc(cW.sem, 16)
                    cW.n += 16
                mark("xbg_1", cW)
        for kc in range(8):
            idx = grp * 8 + kc
            half = idx % 2

            @block.tensor
            def _(t, grp=grp, kc=kc, idx=idx, half=half):
                if kc == 0:
                    t.wait_ge(cW.sem, marks[f"xbg_{grp}"])
                if idx >= 2:
                    t.wait_ge(cV.sem, marks[f"xtev_{idx-2}"])
                last = None
                for tc8 in range(8):
                    last = t.transpose(PTR[:, 1024 * half + 128 * tc8:
                                           1024 * half + 128 * tc8 + 128],
                                       xb[tc8][:, 128 * kc:128 * kc + 128],
                                       idnb[:, :])
                cT.inc(last)
                mark(f"xtr_{idx}", cT)

            @block.vector
            def _(v, kc=kc, grp=grp, idx=idx, half=half):
                v.wait_ge(cT.sem, marks[f"xtr_{idx}"])
                cV.inc(v.tensor_copy(xT[kc][:, 1024 * grp:1024 * grp + 1024],
                                     PTR[:, 1024 * half:1024 * half + 1024]))
                mark(f"xtev_{idx}", cV)
    mark("xT_done", cV)

    def emit_qk_side(b, base, c0):
        kside = (base == 0)

        def emit_phi_round(dc, r):
            i = base + 8 * b + dc

            @block.tensor
            def _(t, dc=dc, i=i, r=r, b=b):
                t.wait_ge(cG.sem, marks[f"grot_{i}"])
                if not kside and b > 0 and dc == 0 and r == 0:
                    t.wait_ge(cV.sem, marks[f"z1_{4*(b-1)+2}"])
                pr = 2 * i + r - 1
                if pr >= 2 * base:
                    t.wait_ge(cS.sem, marks[f"per_{pr}"])
                m = None
                for par in range(2):
                    h = 2 * dc + par
                    lo = 64 * par
                    if kside:
                        for tcs in range(2):
                            tc = 2 * r + tcs
                            m = t.matmul(PS[2 + par][:, 256 * tcs:256 * tcs + 256],
                                         krot[dc][lo:lo + 64, 128 * tc:128 * tc + 128],
                                         mt[lo:lo + 64, 256 * h:256 * h + 256],
                                         start=True, stop=True)
                    else:
                        m = t.matmul(PS[2 + par][:, :],
                                     mt[lo:lo + 64, 256 * h + 128 * r:
                                        256 * h + 128 * r + 128],
                                     krot[dc][lo:lo + 64, :],
                                     start=True, stop=True)
                cT.inc(m)
                mark(f"phr_{2*i+r}", cT)

            @block.scalar
            def _(sc, dc=dc, i=i, r=r):
                sc.wait_ge(cT.sem, marks[f"phr_{2*i+r}"])
                last = None
                for par in range(2):
                    h = 2 * dc + par
                    if kside:
                        dv = AR1[:, 4096 * 2 * r + 256 * h:
                                 4096 * 2 * r + 256 * h + 4352]
                        dv = dv.rearrange("p (a c) -> p a c", a=17)[:, 0:17:16, :]
                        last = sc.activation(
                            dv,
                            PS[2 + par][:, :].rearrange("p (a c) -> p a c", a=2),
                            AF.Exp)
                    else:
                        qi = 2 * (2 * dc + par) + r
                        last = sc.activation(qexp[qi][:, :], PS[2 + par][:, :],
                                             AF.Exp)
                cS.inc(last)
                mark(f"per_{2*i+r}", cS)
            if r == 1:
                marks[f"pe_{i}"] = marks[f"per_{2*i+1}"]

        pending = []
        for dc in range(8):
            i = base + 8 * b + dc

            @block.tensor
            def _(t, b=b, dc=dc, c0=c0, i=i):
                if kside and b == 0 and dc == 0:
                    t.wait_ge(cW.sem, marks["wk"])
                    t.wait_ge(cV.sem, marks["xT_done"])
                if not kside and b == 0 and dc == 0:
                    t.wait_ge(cW.sem, marks["wq"])
                if dc < 2:
                    if b > 0:
                        t.wait_ge(cS.sem, marks[f"vev_{4*(b-1)+3}"] if kside
                                  else marks[f"pe_{base + 8*(b-1) + 7}"])
                        if not kside:
                            t.wait_ge(cV.sem, marks[f"z1_{4*(b-1)+1}"])
                    elif not kside:
                        t.wait_ge(cS.sem, marks["vev_15"])
                        t.wait_ge(cV.sem, marks["sfold_63"])
                if (b, dc) > (0, 1):
                    t.wait_ge(cV.sem, marks[f"rope_{i-2}"])
                m = None
                for kc in range(8):
                    m = t.matmul(PS[dc % 2][:, :], WA[kc][:, 128 * dc:128 * dc + 128],
                                 xT[kc][:, c0:c0 + BLK],
                                 start=(kc == 0), stop=(kc == 7))
                cT.inc(m)
                mark(f"kmm_{i}", cT)

            if pending:
                emit_phi_round(*pending.pop(0))

            @block.vector
            def _(v, dc=dc, c0=c0, i=i):
                v.wait_ge(cT.sem, marks[f"kmm_{i}"])
                if i >= 2:
                    v.wait_ge(cG.sem, marks[f"grot_{i-2}"])
                vchain(v, v.stream_shuffle(ksb[dc % 2][:, :], PS[dc % 2][:, :],
                                           SWAP32))
                cV.inc(v.tensor_tensor(kr2[dc % 2][:, :], PS[dc % 2][:, :],
                                       ropec[:, c0:c0 + BLK], OP.mult))
                mark(f"rope_{i}", cV)

            @block.gpsimd
            def _(g, dc=dc, c0=c0, i=i):
                g.wait_ge(cV.sem, marks[f"rope_{i}"])
                cG.inc(g.tensor_tensor(krot[dc][:, :], ksb[dc % 2][:, :],
                                       ropes[:, c0:c0 + BLK], OP.mult))
                g.wait_ge(cG.sem, cG.n)
                cG.inc(g.tensor_tensor(krot[dc][:, :], krot[dc][:, :],
                                       kr2[dc % 2][:, :], OP.add))
                mark(f"grot_{i}", cG)

            if not kside and dc in (2, 4) and lnq:
                lnq.pop(0)()

            if len(pending) > 2:
                emit_phi_round(*pending.pop(0))
            pending.append((dc, 0))
            pending.append((dc, 1))
        while pending:
            emit_phi_round(*pending.pop(0))

        for dc in range(8):
            i = base + 8 * b + dc

            @block.vector
            def _(v, dc=dc, i=i):
                if dc == 0:
                    v.wait_ge(cV.sem, cV.n)
                if dc < 2:
                    v.wait_ge(cG.sem, marks[f"grot_{i - dc + 7}"])
                if dc >= 2:
                    v.wait_ge(cT.sem, marks[f"nmm_{i-2}"])
                cV.inc(v.tensor_tensor(kr2[dc % 2][:, :], krot[dc][:, :],
                                       krot[dc][:, :], OP.mult))
                mark(f"ksq_{i}", cV)

            @block.tensor
            def _(t, dc=dc, i=i, b=b):
                t.wait_ge(cV.sem, marks[f"ksq_{i}"])
                if dc == 0:
                    if kside and b > 0:
                        t.wait_ge(cV.sem, marks[f"sfold_{16*(b-1)+15}"])
                    if not kside:
                        if b > 0:
                            t.wait_ge(cV.sem, marks[f"z1_{4*(b-1)+3}"])
                        else:
                            t.wait_ge(cV.sem, marks["sfold_63"])
                cT.inc(t.matmul(PS[4][0:16, :], selnq[:, 16 * dc:16 * dc + 16],
                                kr2[dc % 2][:, :], start=(dc == 0), stop=(dc == 7),
                                skip_group_check=True))
                mark(f"nmm_{i}", cT)

    for b in range(NB):
        c0 = BLK * b
        emit_qk_side(b, 0, c0)

        @block.vector
        def _(v, b=b):
            v.wait_ge(cT.sem, marks[f"nmm_{8*b+7}"])
            cV.inc(v.tensor_copy(nkc[:, :], PS[4][0:16, :]))
            mark(f"nkc_{b}", cV)

        for tc in range(4):
            @block.tensor
            def _(t, b=b, tc=tc):
                t.wait_ge(cV.sem, marks[f"nkc_{b}"])
                if b > 0 and tc == 0:
                    t.wait_ge(cV.sem, marks[f"sfold_{16*(b-1)+15}"])
                cT.inc(t.transpose(PS[5][0:128, 32 * tc:32 * tc + 16],
                                   nkc[0:16, 128 * tc:128 * tc + 128],
                                   idnf[0:16, 0:16]))
                mark(f"nkt_{b}_{tc}", cT)

        @block.scalar
        def _(sc, b=b):
            sc.wait_ge(cT.sem, marks[f"nkt_{b}_3"])
            last = None
            for tc in range(4):
                last = sc.activation(wnk[tc][:, 0:16],
                                     PS[5][0:128, 32 * tc:32 * tc + 16],
                                     AF.Exp, bias=0.0, scale=-0.5)
            cS.inc(last)
            mark(f"wnk_{b}", cS)

        for tc in range(4):
            @block.tensor
            def _(t, b=b, tc=tc, c0=c0):
                i = 4 * b + tc
                reg = 2 * (tc % 2)
                if b == 0 and tc == 0:
                    t.wait_ge(cW.sem, marks["wv"])
                t.wait_ge(cS.sem, marks[f"wnk_{b}"])
                if i >= 2:
                    t.wait_ge(cV.sem, marks[f"vaug_{i-2}"])
                tok = c0 + 128 * tc
                m = None
                for kc in range(8):
                    t.matmul(PS[reg][:, :], xT[kc][:, tok:tok + 128],
                             WB[kc][:, 0:512], start=(kc == 0), stop=(kc == 7))
                for kc in range(8):
                    m = t.matmul(PS[reg + 1][:, :], xT[kc][:, tok:tok + 128],
                                 WB[kc][:, 512:1024], start=(kc == 0), stop=(kc == 7))
                cT.inc(m)
                mark(f"vmm_{i}", cT)

            @block.scalar
            def _(sc, b=b, tc=tc):
                i = 4 * b + tc
                reg = 2 * (tc % 2)
                sc.wait_ge(cT.sem, marks[f"vmm_{i}"])
                d0 = vaug[tc][:, 0:1040].rearrange("p (h c) -> p h c", h=16)
                sc.copy(d0[:, 0:8, 0:64],
                        PS[reg][:, :].rearrange("p (h c) -> p h c", h=8))
                cS.inc(sc.copy(d0[:, 8:16, 0:64],
                               PS[reg + 1][:, :].rearrange("p (h c) -> p h c", h=8)))
                mark(f"vev_{i}", cS)

            @block.vector
            def _(v, b=b, tc=tc):
                i = 4 * b + tc
                v.wait_ge(cS.sem, marks[f"vev_{i}"])
                v.wait_ge(cS.sem, marks[f"wnk_{b}"])
                va = vaug[tc][:, 0:1040].rearrange("p (h c) -> p h c", h=16)
                vchain(v, v.memset(va[:, :, 64:65], 1.0))
                wb = wnk[tc][:, 0:16].unsqueeze(2).broadcast_to([128, 16, 65])
                cV.inc(v.tensor_tensor(va, va, wb, OP.mult))
                mark(f"vaug_{i}", cV)

        for h in range(H):
            i = 16 * b + h
            reg = h % 2

            @block.tensor
            def _(t, b=b, h=h, i=i, reg=reg):
                if h < 2:
                    t.wait_ge(cV.sem, marks[f"vaug_{4*b+3}"])
                    if b > 0:
                        t.wait_ge(cV.sem, marks[f"sfold_{16*(b-1)+14+h}"])
                else:
                    t.wait_ge(cV.sem, marks[f"sfold_{i-2}"])
                m = None
                for tc in range(4):
                    m = t.matmul(PS[4 + reg][0:65, 0:256],
                                 vaug[tc][:, 65 * h:65 * h + 65],
                                 kphi[tc][:, 256 * h:256 * h + 256],
                                 start=(tc == 0), stop=(tc == 3))
                cT.inc(m)
                mark(f"smm_{i}", cT)

            @block.vector
            def _(v, b=b, h=h, i=i, reg=reg):
                v.wait_ge(cT.sem, marks[f"smm_{i}"])
                if b == 0:
                    cV.inc(v.tensor_copy(Ssb[h][0:65, :], PS[4 + reg][0:65, 0:256]))
                else:
                    cV.inc(v.tensor_tensor(Ssb[h][0:65, :], PS[4 + reg][0:65, 0:256],
                                           Ssb[h][0:65, :], OP.add))
                mark(f"sfold_{i}", cV)
    mark("S_done", cV)

    @block.gpsimd
    def _(g):
        g.wait_ge(cT.sem, marks["kmm_31"])
        g.wait_ge(cW.sem, cW.n)
        for k in range(8):
            g.dma_start(out=WA[k][:], in_=WQ[128 * k:128 * (k + 1), :]).then_inc(cW.sem, 16)
            cW.n += 16
        mark("wq", cW)
        g.wait_ge(cT.sem, marks["vmm_15"])
        g.wait_ge(cW.sem, cW.n)
        for k in range(8):
            g.dma_start(out=WB[k][:], in_=WO[128 * k:128 * (k + 1), :]).then_inc(cW.sem, 16)
            cW.n += 16
        mark("wo", cW)
        g.wait_ge(cV.sem, marks["S_done"])
        g.wait_ge(cW.sem, cW.n)
        for h in range(H):
            g.dma_start(out=SBIN.ap()[128 * h:128 * (h + 1), :],
                        in_=Ssb[h][:, :]).then_inc(cW.sem, 16)
            cW.n += 16
        mark("sbin", cW)
        g.wait_ge(cW.sem, marks["sbin"])
        cC.inc(g.collective_compute(
            "AllReduce", OP.add,
            replica_groups=[[0, 1, 2, 3], [4, 5, 6, 7]],
            ins=[SBIN.ap().opt()], outs=[SBOUT.ap().opt()]))
        mark("s_ar", cC)

    def emit_sred():
        @block.gpsimd
        def _(g):
            g.wait_ge(cC.sem, marks["s_ar"])
            g.wait_ge(cW.sem, cW.n)
            for h in range(H):
                g.dma_start(out=Ssb[h][:, :],
                            in_=SBOUT.ap()[128 * h:128 * (h + 1), :]
                            ).then_inc(cW.sem, 16)
                cW.n += 16
            mark("sred", cW)

    PTF2A = PTR[:, :].bitcast(F32)
    lnq = []

    def emit_ytt(i):
        @block.tensor
        def _(t, i=i):
            t.wait_ge(cV.sem, marks[f"lnapply_{i}"])
            if i >= 1:
                t.wait_ge(cV.sem, marks[f"ytev2_{i-1}"])
            else:
                t.wait_ge(cV.sem, marks["xT_done"])
            last = None
            for kc in range(8):
                last = t.transpose(
                    PTF2A[:, 128 * kc:128 * kc + 128],
                    ytr[i % 3][:, 128 * kc:128 * kc + 128], idnf[:, :])
            cT.inc(last)
            mark(f"ytt_{i}", cT)

        @block.vector
        def _(v, i=i):
            v.wait_ge(cT.sem, marks[f"ytt_{i}"])
            cV.inc(v.tensor_copy(
                XTB.rearrange("p (a t) -> p a t", a=8)[:, 0:8,
                                                       128 * i:128 * i + 128],
                PTF2A[:, 0:1024].rearrange("p (a t) -> p a t", a=8)))
            mark(f"ytev2_{i}", cV)

    for b in range(NB):
        c0 = BLK * b
        emit_qk_side(b, 32, c0)
        if b == 0:
            emit_sred()
        if b == 3:
            @block.gpsimd
            def _(g):
                g.wait_ge(cT.sem, marks["kmm_63"])
                g.wait_ge(cW.sem, cW.n)
                for k in range(8):
                    g.dma_start(out=WA[k][:],
                                in_=WR[128 * k:128 * (k + 1), :]).then_inc(cW.sem, 16)
                    cW.n += 16
                mark("wr", cW)

        if b == 0:
            for h in range(H):
                @block.tensor
                def _(t, h=h):
                    if h == 0:
                        t.wait_ge(cW.sem, marks["sred"])
                        t.wait_ge(cV.sem, marks["rope_39"])
                    if h >= 2:
                        t.wait_ge(cV.sem, marks[f"sev_{h-2}"])
                    t.transpose(PS[h % 2][:, 0:128], Ssb[h][:, 0:128], idnf[:, :])
                    cT.inc(t.transpose(PS[h % 2][:, 128:256], Ssb[h][:, 128:256],
                                       idnf[:, :]))
                    mark(f"str_{h}", cT)

                @block.vector
                def _(v, h=h):
                    v.wait_ge(cT.sem, marks[f"str_{h}"])
                    for m in range(2):
                        v.tensor_copy(saug[m][:, 65 * h:65 * h + 65],
                                      PS[h % 2][:, 128 * m:128 * m + 65])
                    v.tensor_copy(skc[0][:, h:h + 1], PS[h % 2][:, 64:65])
                    cV.inc(v.tensor_copy(skc[1][:, h:h + 1], PS[h % 2][:, 192:193]))
                    mark(f"sev_{h}", cV)
            mark("saug_done", cV)

        @block.scalar
        def _(sc, b=b):
            sc.wait_ge(cT.sem, marks[f"nmm_{32+8*b+7}"])
            if b > 0:
                sc.wait_ge(cV.sem, marks[f"g_{b-1}"])
            cS.inc(sc.activation(enq[:, :], PS[4][0:16, :], AF.Exp,
                                 bias=0.0, scale=-0.5))
            mark(f"enq_{b}", cS)

        @block.tensor
        def _(t, b=b):
            t.wait_ge(cS.sem, marks[f"pe_{32+8*b+7}"])
            t.wait_ge(cS.sem, marks[f"enq_{b}"])
            if b > 0:
                t.wait_ge(cV.sem, marks[f"z1_{4*(b-1)+3}"])
            else:
                t.wait_ge(cV.sem, marks["saug_done"])
            first = True
            m = None
            for h in range(H):
                for mc in range(2):
                    m = t.matmul(PS[5][0:16, :], skc[mc][:, :],
                                 qexp[2 * h + mc][:, :],
                                 start=first, stop=(h == H - 1 and mc == 1))
                    first = False
            cT.inc(m)
            mark(f"draw_{b}", cT)

        @block.vector
        def _(v, b=b):
            v.wait_ge(cT.sem, marks[f"draw_{b}"])
            v.wait_ge(cS.sem, marks[f"enq_{b}"])
            vchain(v, v.tensor_tensor(gch[0][:, :], PS[5][0:16, :], enq[:, :],
                                      OP.mult))
            vchain(v, v.tensor_scalar(gch[1][:, :], gch[0][:, :], MEPS, None,
                                      OP.max))
            vchain(v, v.reciprocal(gch[0][:, :], gch[1][:, :]))
            cV.inc(v.tensor_tensor(gsb[:, :], gch[0][:, :], enq[:, :], OP.mult))
            mark(f"g_{b}", cV)

        for p in range(8):
            i = 8 * b + p
            reg = p % 2

            @block.tensor
            def _(t, b=b, p=p, i=i, reg=reg):
                t.wait_ge(cV.sem, marks[f"g_{b}"])
                if p >= 2:
                    t.wait_ge(cV.sem, marks[f"oh_{i-2}"])
                elif b > 0:
                    t.wait_ge(cV.sem, marks[f"oh_{8*(b-1)+6+p}"])
                t.matmul(PS[2 + reg][:, :], selg[:, 128 * p:128 * p + 128],
                         gsb[:, :], start=True, stop=True)
                m = None
                for par in range(2):
                    h = 2 * p + par
                    lo = 64 * par
                    for mc in range(2):
                        m = t.matmul(PS[reg][lo:lo + 64, :],
                                     saug[mc][:, 65 * h:65 * h + 64],
                                     qexp[2 * h + mc][:, :],
                                     start=(mc == 0), stop=(mc == 1))
                cT.inc(m)
                mark(f"numer_{i}", cT)

            @block.scalar
            def _(sc, i=i, reg=reg):
                sc.wait_ge(cT.sem, marks[f"numer_{i}"])
                cS.inc(sc.copy(gx[reg][:, :], PS[2 + reg][:, :]))
                mark(f"gxe_{i}", cS)

            @block.vector
            def _(v, p=p, i=i, reg=reg):
                v.wait_ge(cS.sem, marks[f"gxe_{i}"])
                cV.inc(v.tensor_tensor(oh[p][:, :], PS[reg][:, :], gx[reg][:, :],
                                       OP.mult))
                mark(f"oh_{i}", cV)

        for tc in range(4):
            i = 4 * b + tc
            if i >= 3:
                emit_ytt(i - 3)

            @block.sync
            def _(s, b=b, tc=tc, i=i):
                if i < 2:
                    s.wait_ge(cT.sem, marks["smm_63"])
                else:
                    s.wait_ge(cV.sem, marks[f"z1_{i-2}"])
                s.wait_ge(cXF[i % 2].sem, cXF[i % 2].n)
                s.dma_start(out=xf2[tc % 2][:, :],
                            in_=X[BLK * b + 128 * tc:BLK * b + 128 * tc + 128, :]
                            ).then_inc(cXF[i % 2].sem, 16)
                cXF[i % 2].n += 16
                mark(f"xf_{i}", cXF[i % 2])

            @block.tensor
            def _(t, b=b, tc=tc, i=i):
                t.wait_ge(cV.sem, marks[f"oh_{8*b+7}"])
                reg = [4, 0, 2, 4][tc]
                if b == 0 and tc == 0:
                    t.wait_ge(cW.sem, marks["wo"])
                if tc == 0 and b > 0:
                    t.wait_ge(cV.sem, marks[f"z1_{4*(b-1)+3}"])
                elif tc == 3:
                    t.wait_ge(cV.sem, marks[f"z1_{4*b}"])
                m = None
                for p in range(8):
                    t.matmul(PS[reg][:, :], oh[p][:, 128 * tc:128 * tc + 128],
                             WB[p][:, 0:512], start=(p == 0), stop=(p == 7))
                for p in range(8):
                    m = t.matmul(PS[reg + 1][:, :], oh[p][:, 128 * tc:128 * tc + 128],
                                 WB[p][:, 512:1024], start=(p == 0), stop=(p == 7))
                cT.inc(m)
                mark(f"oproj_{i}", cT)

            @block.vector
            def _(v, tc=tc, i=i):
                v.wait_ge(cT.sem, marks[f"oproj_{i}"])
                v.wait_ge(cXF[i % 2].sem, marks[f"xf_{i}"])
                reg = [4, 0, 2, 4][tc]
                if i >= 2:
                    v.wait_ge(cS.sem, marks[f"z1sq_{i-2}"])
                v.tensor_tensor(z1[i % 2][:, 0:512], PS[reg][:, :],
                                xf2[tc % 2][:, 0:512], OP.add)
                vchain(v, v.tensor_tensor(z1[i % 2][:, 512:1024], PS[reg + 1][:, :],
                                          xf2[tc % 2][:, 512:1024], OP.add))
                cV.inc(v.reduce_sum(stat[:, 8 * (i % 2):8 * (i % 2) + 1],
                                    z1[i % 2][:, :], axis=mybir.AxisListType.X))
                mark(f"z1_{i}", cV)

            def ln_tail(b=b, tc=tc, i=i):
                so = 8 * (i % 2)

                @block.scalar
                def _(sc, i=i, so=so):
                    sc.wait_ge(cV.sem, marks[f"z1_{i}"])
                    cS.inc(sc.activation(sqs[:, :], z1[i % 2][:, :], AF.Square,
                                         accum_out=stat[:, so + 1:so + 2]))
                    mark(f"z1sq_{i}", cS)

                @block.vector
                def _(v, i=i, so=so):
                    v.wait_ge(cS.sem, marks[f"z1sq_{i}"])
                    ci = 1.0 / E
                    vchain(v, v.tensor_scalar(stat[:, so + 2:so + 3],
                                              stat[:, so:so + 1], ci, None,
                                              OP.mult))
                    vchain(v, v.tensor_tensor(stat[:, so + 3:so + 4],
                                              stat[:, so + 2:so + 3],
                                              stat[:, so + 2:so + 3], OP.mult))
                    vchain(v, v.tensor_scalar(stat[:, so + 3:so + 4],
                                              stat[:, so + 3:so + 4], -1.0,
                                              LNEPS, OP.mult, OP.add))
                    cV.inc(v.scalar_tensor_tensor(stat[:, so + 4:so + 5],
                                                  stat[:, so + 1:so + 2], ci,
                                                  stat[:, so + 3:so + 4],
                                                  OP.mult, OP.add))
                    mark(f"var_{i}", cV)

                @block.scalar
                def _(sc, i=i, so=so):
                    sc.wait_ge(cV.sem, marks[f"var_{i}"])
                    cS.inc(sc.activation(stat[:, so + 5:so + 6],
                                         stat[:, so + 4:so + 5], AF.Sqrt))
                    mark(f"sd_{i}", cS)

                @block.vector
                def _(v, i=i, so=so):
                    v.wait_ge(cS.sem, marks[f"sd_{i}"])
                    if i >= 3:
                        v.wait_ge(cYD[(i - 3) % 2].sem,
                                  marks[f"ydma_{i-3}"])
                        v.wait_ge(cT.sem, marks[f"ytt_{i-3}"])
                    vchain(v, v.reciprocal(stat[:, so + 6:so + 7],
                                           stat[:, so + 5:so + 6]))
                    vchain(v, v.scalar_tensor_tensor(stat[:, so + 7:so + 8],
                                                     stat[:, so + 2:so + 3],
                                                     -1.0, stat[:, so + 6:so + 7],
                                                     OP.mult, OP.mult))
                    cV.inc(v.tensor_scalar(ytr[i % 3][:, :], z1[i % 2][:, :],
                                           stat[:, so + 6:so + 7],
                                           stat[:, so + 7:so + 8],
                                           OP.mult, OP.add))
                    mark(f"lnapply_{i}", cV)

                @block.sync
                def _(s, b=b, tc=tc, i=i):
                    s.wait_ge(cV.sem, marks[f"lnapply_{i}"])
                    s.wait_ge(cYD[i % 2].sem, cYD[i % 2].n)
                    s.dma_start(out=YBUF.ap()[BLK * b + 128 * tc:
                                              BLK * b + 128 * tc + 128, :],
                                in_=ytr[i % 3][:, :]).then_inc(cYD[i % 2].sem, 16)
                    cYD[i % 2].n += 16
                    mark(f"ydma_{i}", cYD[i % 2])

            if tc < 2:
                ln_tail()
            else:
                lnq.append(ln_tail)

    while lnq:
        lnq.pop(0)()
    marks["y_done0"] = cYD[0].n
    marks["y_done1"] = cYD[1].n
    emit_ytt(13)
    emit_ytt(14)
    emit_ytt(15)

    if debug:
        @block.sync
        def _(s):
            s.wait_ge(cV.sem, cV.n)
            s.wait_ge(cS.sem, cS.n)
            s.wait_ge(cYD[0].sem, marks["y_done0"])
            s.wait_ge(cYD[1].sem, marks["y_done1"])
            s.wait_ge(cX.sem, cX.n)
            for nm, srcap in (("xT0", xT[0][:, :]), ("krot0", krot[0][:, :]),
                              ("kphi00", kphi[0][:, :]), ("w00", wnk[0][:, :]),
                              ("saug0", saug[0][:, :]), ("enq0", enq[:, :]),
                              ("g0", gsb[:, :]), ("oh0", oh[0][:, :]),
                              ("y0", YBUF.ap()[0:128, :])):
                s.dma_start(out=taps[nm], in_=srcap).then_inc(cX.sem, 16)
                cX.n += 16
            mark("tapmid", cX)

    @block.gpsimd
    def _(g):
        g.wait_ge(cT.sem, marks["oproj_15"])
        g.wait_ge(cW.sem, cW.n)
        for k in range(8):
            g.dma_start(out=WB[k][:], in_=WKF[128 * k:128 * (k + 1), :]).then_inc(cW.sem, 16)
            cW.n += 16
        mark("wkf", cW)
        g.wait_ge(cW.sem, cW.n)
        for k in range(8):
            g.dma_start(out=WC[k][:], in_=WVF[128 * k:128 * (k + 1), :]).then_inc(cW.sem, 16)
            cW.n += 16
        mark("wvf", cW)

    @block.sync
    def _(s):
        s.wait_ge(cYD[0].sem, marks["y_done0"])
        s.wait_ge(cYD[1].sem, marks["y_done1"])
        s.wait_ge(cX.sem, cX.n)
        s.dma_start(out=GIN.ap(), in_=ytr[15 % 3][127:128, :]).then_inc(cX.sem, 16)
        cX.n += 16
        mark("gin", cX)

    @block.gpsimd
    def _(g):
        g.wait_ge(cX.sem, marks["gin"])
        g.wait_ge(cYD[0].sem, marks["y_done0"])
        g.wait_ge(cYD[1].sem, marks["y_done1"])
        cC.inc(g.collective_compute(
            "AllGather", OP.bypass,
            replica_groups=[[0, 1, 2, 3, 4, 5, 6, 7]],
            ins=[GIN.ap().opt()], outs=[GOUT.ap().opt()]))
        mark("ag", cC)
        g.wait_ge(cC.sem, marks["ag"])
        g.wait_ge(cW.sem, cW.n)
        g.dma_start(out=agsb[:, :], in_=GOUT.ap()).then_inc(cW.sem, 16)
        cW.n += 16
        mark("agld", cW)

    PB = [1, 2, 3, 0]
    PRP = [0, 2, 4, 0]
    PWP = [4, 0, 2, 4]

    PTF = PTR[:, :].bitcast(F32)
    lnq2 = []
    for bi, b in enumerate(PB):
        c0 = BLK * b
        prev = PB[bi - 1] if bi > 0 else None
        if bi == 1:
            @block.tensor
            def _(t):
                t.wait_ge(cW.sem, marks["agld"])
                t.wait_ge(cV.sem, marks["ytev2_15"])
                t.matmul(PTF[0:1, 0:512], hsel[:, :], agsb[:, 0:512],
                         start=True, stop=True)
                cT.inc(t.matmul(PTF[0:1, 512:1024], hsel[:, :], agsb[:, 512:1024],
                                start=True, stop=True))
                mark("halomm", cT)

            @block.scalar
            def _(sc):
                sc.wait_ge(cT.sem, marks["halomm"])
                sc.copy(hrow[0:1, 0:512], PTF[0:1, 0:512])
                cS.inc(sc.copy(hrow[0:1, 512:1024], PTF[0:1, 512:1024]))
                mark("hrow", cS)

            @block.tensor
            def _(t):
                t.wait_ge(cS.sem, marks["hrow"])
                last = None
                for k in range(8):
                    last = t.matmul(PTF[:, k:k + 1],
                                    hrow[0:1, 128 * k:128 * k + 128],
                                    ones11[0:1, 0:1], start=True, stop=True)
                cT.inc(last)
                mark("halot", cT)

            @block.vector
            def _(v):
                v.wait_ge(cT.sem, marks["halot"])
                cV.inc(v.tensor_copy(haloT[:, 0:8], PTF[:, 0:8]))
                mark("haloT", cV)
        for ec in range(8):
            @block.vector
            def _(v, b=b, ec=ec, c0=c0, prev=prev):
                v.wait_ge(cV.sem, marks[f"ytev2_{4*b+3}"])
                v.wait_ge(cV.sem, cV.n)
                if b == 0 and ec == 0:
                    v.wait_ge(cV.sem, marks["haloT"])
                if prev is not None and ec == 0:
                    v.wait_ge(cT.sem, marks[f"rmm_{4*prev+3}"])
                if c0 == 0:
                    v.tensor_tensor(dsb[:, 1:BLK], yT[ec][:, 1:BLK],
                                    yT[ec][:, 0:BLK - 1], OP.subtract)
                    vchain(v, v.tensor_tensor(dsb[:, 0:1], yT[ec][:, 0:1],
                                              haloT[:, ec:ec + 1], OP.subtract))
                    v.scalar_tensor_tensor(xk[ec][:, 1:BLK], dsb[:, 1:BLK],
                                           muk[:, ec:ec + 1], yT[ec][:, 0:BLK - 1],
                                           OP.mult, OP.add)
                    cV.inc(v.scalar_tensor_tensor(xk[ec][:, 0:1], dsb[:, 0:1],
                                                  muk[:, ec:ec + 1],
                                                  haloT[:, ec:ec + 1],
                                                  OP.mult, OP.add))
                    if ec == 7:
                        mark(f"mixk_{b}", cV)
                    v.scalar_tensor_tensor(xr[ec][:, 1:BLK], dsb[:, 1:BLK],
                                           mur[:, ec:ec + 1], yT[ec][:, 0:BLK - 1],
                                           OP.mult, OP.add)
                    cV.inc(v.scalar_tensor_tensor(xr[ec][:, 0:1], dsb[:, 0:1],
                                                  mur[:, ec:ec + 1],
                                                  haloT[:, ec:ec + 1],
                                                  OP.mult, OP.add))
                else:
                    vchain(v, v.tensor_tensor(dsb[:, :], yT[ec][:, c0:c0 + BLK],
                                              yT[ec][:, c0 - 1:c0 + BLK - 1],
                                              OP.subtract))
                    cV.inc(v.scalar_tensor_tensor(xk[ec][:, :], dsb[:, :],
                                                  muk[:, ec:ec + 1],
                                                  yT[ec][:, c0 - 1:c0 + BLK - 1],
                                                  OP.mult, OP.add))
                    if ec == 7:
                        mark(f"mixk_{b}", cV)
                    cV.inc(v.scalar_tensor_tensor(xr[ec][:, :], dsb[:, :],
                                                  mur[:, ec:ec + 1],
                                                  yT[ec][:, c0 - 1:c0 + BLK - 1],
                                                  OP.mult, OP.add))
                mark(f"mix_{8*b+ec}", cV)
        mark(f"mixb_{b}", cV)
        while lnq2:
            lnq2.pop(0)()

        for dc in range(8):
            i = 8 * b + dc

            @block.tensor
            def _(t, b=b, dc=dc, i=i, bi=bi, prev=prev):
                t.wait_ge(cV.sem, marks[f"mixk_{b}"])
                if bi == 0 and dc == 0:
                    t.wait_ge(cW.sem, marks["wkf"])
                if dc < 2 and prev is not None:
                    t.wait_ge(cV.sem, marks[f"z2_{4*prev+2}"])
                if dc >= 2:
                    t.wait_ge(cV.sem, marks[f"kasq_{i-2}"])
                m = None
                for ec in range(8):
                    m = t.matmul(PS[2 + dc % 2][:, :],
                                 WB[ec][:, 128 * dc:128 * dc + 128],
                                 xk[ec][:, :], start=(ec == 0), stop=(ec == 7))
                cT.inc(m)
                mark(f"kamm_{i}", cT)

            @block.vector
            def _(v, dc=dc, i=i):
                v.wait_ge(cT.sem, marks[f"kamm_{i}"])
                vchain(v, v.tensor_scalar(kr2[dc % 2][:, :], PS[2 + dc % 2][:, :],
                                          0.0, None, OP.max))
                cV.inc(v.tensor_tensor(kact[dc][:, :], kr2[dc % 2][:, :],
                                       kr2[dc % 2][:, :], OP.mult))
                mark(f"kasq_{i}", cV)

        for tc in range(4):
            i = 4 * b + tc
            pr = PRP[tc]
            pw = PWP[tc]

            @block.tensor
            def _(t, b=b, tc=tc, i=i, pr=pr, bi=bi, prev=prev):
                t.wait_ge(cV.sem, marks[f"mixb_{b}"])
                if bi == 0 and tc == 0:
                    t.wait_ge(cW.sem, marks["wr"])
                if tc == 0:
                    if prev is not None:
                        t.wait_ge(cS.sem, marks[f"rsig_{4*prev+3}"])
                elif tc == 1:
                    t.wait_ge(cV.sem, marks[f"kasq_{8*b+7}"])
                elif tc == 2:
                    t.wait_ge(cV.sem, marks[f"z2_{4*b+0}"])
                else:
                    t.wait_ge(cV.sem, marks[f"z2_{4*b+1}"])
                m = None
                for ec in range(8):
                    t.matmul(PS[pr][:, :], xr[ec][:, 128 * tc:128 * tc + 128],
                             WA[ec][:, 0:512], start=(ec == 0), stop=(ec == 7))
                for ec in range(8):
                    m = t.matmul(PS[pr + 1][:, :], xr[ec][:, 128 * tc:128 * tc + 128],
                                 WA[ec][:, 512:1024], start=(ec == 0), stop=(ec == 7))
                cT.inc(m)
                mark(f"rmm_{i}", cT)

            @block.scalar
            def _(sc, b=b, tc=tc, i=i, pr=pr, bi=bi, prev=prev):
                sc.wait_ge(cT.sem, marks[f"rmm_{i}"])
                if tc >= 2:
                    sc.wait_ge(cV.sem, marks[f"z2_{4*b+tc-2}"])
                elif prev is not None:
                    sc.wait_ge(cV.sem, marks[f"z2_{4*prev+2+tc}"])
                sc.activation(rsb[tc % 2][:, 0:512], PS[pr][:, :], AF.Sigmoid)
                cS.inc(sc.activation(rsb[tc % 2][:, 512:1024], PS[pr + 1][:, :],
                                     AF.Sigmoid))
                mark(f"rsig_{i}", cS)

            @block.tensor
            def _(t, b=b, tc=tc, i=i, pw=pw, bi=bi, prev=prev):
                t.wait_ge(cV.sem, marks[f"kasq_{8*b+7}"])
                if bi == 0 and tc == 0:
                    t.wait_ge(cW.sem, marks["wvf"])
                if tc == 0:
                    if prev is not None:
                        t.wait_ge(cV.sem, marks[f"z2_{4*prev+3}"])
                elif tc == 1:
                    t.wait_ge(cS.sem, marks[f"rsig_{4*b+0}"])
                elif tc == 2:
                    t.wait_ge(cS.sem, marks[f"rsig_{4*b+1}"])
                else:
                    t.wait_ge(cS.sem, marks[f"rsig_{4*b+2}"])
                m = None
                for dc in range(8):
                    t.matmul(PS[pw][:, :], kact[dc][:, 128 * tc:128 * tc + 128],
                             WC[dc][:, 0:512], start=(dc == 0), stop=(dc == 7))
                for dc in range(8):
                    m = t.matmul(PS[pw + 1][:, :], kact[dc][:, 128 * tc:128 * tc + 128],
                                 WC[dc][:, 512:1024], start=(dc == 0), stop=(dc == 7))
                cT.inc(m)
                mark(f"wvmm_{i}", cT)

            @block.sync
            def _(s, b=b, tc=tc, i=i, bi=bi, prev=prev):
                if tc >= 2:
                    s.wait_ge(cV.sem, marks[f"z2_{4*b+tc-2}"])
                elif prev is not None:
                    s.wait_ge(cV.sem, marks[f"z2_{4*prev+2+tc}"])
                s.wait_ge(cY2[i % 2].sem, cY2[i % 2].n)
                s.dma_start(out=y2[i % 2][:, :],
                            in_=YBUF.ap()[BLK * b + 128 * tc:BLK * b + 128 * tc + 128, :]
                            ).then_inc(cY2[i % 2].sem, 16)
                cY2[i % 2].n += 16
                mark(f"y2_{i}", cY2[i % 2])

            @block.vector
            def _(v, tc=tc, i=i, pw=pw):
                v.wait_ge(cT.sem, marks[f"wvmm_{i}"])
                v.wait_ge(cS.sem, marks[f"rsig_{i}"])
                v.wait_ge(cY2[i % 2].sem, marks[f"y2_{i}"])
                v.wait_ge(cX.sem, marks["gin"])
                if i == 4:
                    v.wait_ge(cT.sem, marks["ytt_15"])
                elif i == 5:
                    v.wait_ge(cT.sem, marks["ytt_13"])
                v.tensor_tensor(tmul[i % 2][:, 0:512], PS[pw][:, :],
                                rsb[tc % 2][:, 0:512], OP.mult)
                vchain(v, v.tensor_tensor(tmul[i % 2][:, 512:1024], PS[pw + 1][:, :],
                                          rsb[tc % 2][:, 512:1024], OP.mult))
                vchain(v, v.tensor_tensor(z2[i % 2][:, :], tmul[i % 2][:, :],
                                          y2[i % 2][:, :], OP.add))
                cV.inc(v.reduce_sum(stat[:, 8 * (i % 2):8 * (i % 2) + 1],
                                    z2[i % 2][:, :], axis=mybir.AxisListType.X))
                mark(f"z2_{i}", cV)

            def ln2_tail(b=b, tc=tc, i=i, prev=prev):
                so = 8 * (i % 2)

                @block.scalar
                def _(sc, i=i, so=so):
                    sc.wait_ge(cV.sem, marks[f"z2_{i}"])
                    cS.inc(sc.activation(sqs[:, :], z2[i % 2][:, :], AF.Square,
                                         accum_out=stat[:, so + 1:so + 2]))
                    mark(f"z2sq_{i}", cS)

                @block.vector
                def _(v, i=i, so=so):
                    v.wait_ge(cS.sem, marks[f"z2sq_{i}"])
                    ci = 1.0 / E
                    vchain(v, v.tensor_scalar(stat[:, so + 2:so + 3],
                                              stat[:, so:so + 1], ci, None,
                                              OP.mult))
                    vchain(v, v.tensor_tensor(stat[:, so + 3:so + 4],
                                              stat[:, so + 2:so + 3],
                                              stat[:, so + 2:so + 3], OP.mult))
                    vchain(v, v.tensor_scalar(stat[:, so + 3:so + 4],
                                              stat[:, so + 3:so + 4], -1.0,
                                              LNEPS, OP.mult, OP.add))
                    cV.inc(v.scalar_tensor_tensor(stat[:, so + 4:so + 5],
                                                  stat[:, so + 1:so + 2], ci,
                                                  stat[:, so + 3:so + 4],
                                                  OP.mult, OP.add))
                    mark(f"var2_{i}", cV)

                @block.scalar
                def _(sc, i=i, so=so):
                    sc.wait_ge(cV.sem, marks[f"var2_{i}"])
                    cS.inc(sc.activation(stat[:, so + 5:so + 6],
                                         stat[:, so + 4:so + 5], AF.Sqrt))
                    mark(f"sd2_{i}", cS)

                @block.vector
                def _(v, i=i, b=b, tc=tc, prev=prev, so=so):
                    v.wait_ge(cS.sem, marks[f"sd2_{i}"])
                    if tc >= 2:
                        v.wait_ge(cOD[i % 2].sem, marks[f"odma_{4*b+tc-2}"])
                    elif prev is not None:
                        v.wait_ge(cOD[(4*prev+2+tc) % 2].sem,
                                  marks[f"odma_{4*prev+2+tc}"])
                    vchain(v, v.reciprocal(stat[:, so + 6:so + 7],
                                           stat[:, so + 5:so + 6]))
                    vchain(v, v.scalar_tensor_tensor(stat[:, so + 7:so + 8],
                                                     stat[:, so + 2:so + 3],
                                                     -1.0, stat[:, so + 6:so + 7],
                                                     OP.mult, OP.mult))
                    cV.inc(v.tensor_scalar(osb[i % 2][:, :], z2[i % 2][:, :],
                                           stat[:, so + 6:so + 7],
                                           stat[:, so + 7:so + 8],
                                           OP.mult, OP.add))
                    mark(f"lnout_{i}", cV)

                @block.sync
                def _(s, b=b, tc=tc, i=i):
                    s.wait_ge(cV.sem, marks[f"lnout_{i}"])
                    s.wait_ge(cOD[i % 2].sem, cOD[i % 2].n)
                    s.dma_start(out=OUT[BLK * b + 128 * tc:
                                        BLK * b + 128 * tc + 128, :],
                                in_=osb[i % 2][:, :]).then_inc(cOD[i % 2].sem, 16)
                    cOD[i % 2].n += 16
                    mark(f"odma_{i}", cOD[i % 2])

            if tc < 2:
                ln2_tail()
            else:
                lnq2.append(ln2_tail)

    while lnq2:
        lnq2.pop(0)()

    if debug:
        @block.sync
        def _(s):
            s.wait_ge(cOD[0].sem, cOD[0].n)
            s.wait_ge(cOD[1].sem, cOD[1].n)
            s.wait_ge(cV.sem, cV.n)
            s.wait_ge(cS.sem, cS.n)
            s.wait_ge(cX.sem, cX.n)
            for nm, srcap in (("mt", mt[:, :]), ("xr0", xr[0][:, :]),
                              ("kact0", kact[0][:, :]), ("z20", z2[0][:, :])):
                s.dma_start(out=taps[nm], in_=srcap).then_inc(cX.sem, 16)
                cX.n += 16

    @block.sync
    def _(s):
        s.wait_ge(cOD[0].sem, cOD[0].n)
        s.wait_ge(cOD[1].sem, cOD[1].n)
        s.wait_ge(cX.sem, cX.n)

    return nc


def _in_maps(inputs):
    x = np.ascontiguousarray(np.asarray(inputs["L0_attn_x"], dtype=np.float32))
    w_qkv = np.asarray(inputs["L0_attn_w_qkv"], dtype=np.float32)
    w_out = np.asarray(inputs["L0_attn_w_out"], dtype=np.float32)
    signs = np.asarray(inputs["L0_attn_sorf_signs"], dtype=np.float32)
    scales = np.asarray(inputs["L0_attn_sorf_scales"], dtype=np.float32)
    W_r = np.asarray(inputs["L0_ffn_W_r"], dtype=np.float32)
    W_k = np.asarray(inputs["L0_ffn_W_k"], dtype=np.float32)
    W_v = np.asarray(inputs["L0_ffn_W_v"], dtype=np.float32)
    mu_r = np.asarray(inputs["L0_ffn_mu_r"], dtype=np.float32)
    mu_k = np.asarray(inputs["L0_ffn_mu_k"], dtype=np.float32)

    consts = _host_consts()
    shared = {
        "wq": np.ascontiguousarray(w_qkv[0:E, :].T),
        "wk": np.ascontiguousarray(w_qkv[E:2 * E, :].T),
        "wv": np.ascontiguousarray(w_qkv[2 * E:3 * E, :].T),
        "wo": np.ascontiguousarray(w_out.T),
        "wr": np.ascontiguousarray(W_r.T),
        "wkf": np.ascontiguousarray(W_k.T),
        "wvf": np.ascontiguousarray(W_v.T),
        "mur": np.ascontiguousarray(mu_r.reshape(8, 128).T),
        "muk": np.ascontiguousarray(mu_k.reshape(8, 128).T),
        "scl": np.ascontiguousarray(np.hstack([scales.reshape(64, 64)] * 2)),
        "h64": consts["h64"], "i64rep": consts["i64rep"],
        "idnb": consts["idnb"], "idnf": consts["idnf"],
        "selnq": consts["selnq"], "selg": consts["selg"],
    }
    for s in range(3):
        shared[f"sg{s+1}"] = np.ascontiguousarray(
            np.hstack([signs[:, :, s, :].reshape(64, 64)] * 2))

    in_maps = []
    for i in range(NCORES):
        c, j = divmod(i, 4)
        t0 = j * T
        Ct, St = _rope_tables(t0)
        hs = np.zeros((8, 1), dtype=np.float32)
        if j > 0:
            hs[i - 1, 0] = 1.0
        m = dict(shared)
        m["x"] = np.ascontiguousarray(x[c, t0:t0 + T, :])
        m["ropec"] = Ct
        m["ropes"] = St
        m["hsel"] = hs
        in_maps.append(m)
    return in_maps


def kernel(**inputs):
    in_maps = _in_maps(inputs)
    nc = build_nc(debug=False)
    res = run_bass_kernel_spmd(nc, in_maps, core_ids=list(range(NCORES)))
    out = np.empty((2, 8192, E), dtype=np.float32)
    for i in range(NCORES):
        c, j = divmod(i, 4)
        out[c, j * T:(j + 1) * T, :] = res.results[i]["out"]
    return out
